# version 22
# baseline (speedup 1.0000x reference)
import numpy as np
import ml_dtypes
import concourse.bass as bass
import concourse.mybir as mybir
from concourse.bass_utils import run_bass_kernel_spmd

F32 = mybir.dt.float32
BF16 = mybir.dt.bfloat16
AF = mybir.ActivationFunctionType
ALU = mybir.AluOpType
AX = mybir.AxisListType
BF16NP = ml_dtypes.bfloat16

B, S, D = 4, 2048, 4096
NH, KVH, HD = 32, 8, 128
GROUPS = NH // KVH
NKC = D // 128
CH = 256
OWN = S // 2
E_, R_ = 8, 16
ER = E_ * R_
SCALE = float(1.0 / np.sqrt(HD))
EPS = 1e-5
ROPE_THETA = 10000.0
INV_D = 1.0 / D

ENGS = ("sync", "gpsimd", "tensor", "scalar", "vector")
P3_PIPE = True
USE_SCALARQ = True
DEFER0 = 3
DEFER1 = 3
SEQ_LN = True

SORT8 = [(0, 1), (2, 3), (4, 5), (6, 7), (0, 2), (1, 3), (4, 6), (5, 7),
         (1, 2), (5, 6), (0, 4), (1, 5), (2, 6), (3, 7), (2, 4), (3, 5),
         (1, 2), (3, 4), (5, 6)]


class Sem:
    def __init__(self, nc, name):
        self.h = nc.alloc_semaphore(name)
        self.name = name
        self.n = 0


class Builder:

    def __init__(self, nc):
        self.nc = nc
        self.q = {e: [] for e in ENGS}
        self.sems = {}
        for e in ENGS:
            self.sems[e] = Sem(nc, "sem_" + e)
        self.sems["scalarq"] = Sem(nc, "sem_scalarq")
        self.esem = Sem(nc, "sem_end")
        self.last_write = {}
        self.readers = {}
        self.satisfied = {e: {} for e in ENGS}
        self.pending = {e: [] for e in ENGS}
        self.log = []

    @staticmethod
    def key(t):
        return t if isinstance(t, str) else t.name

    def op(self, eng, fn, reads=(), writes=(), inc=True, extra=(), is_dma=None,
           stream=None):
        if is_dma is None:
            is_dma = eng in ("sync", "gpsimd")
        if is_dma:
            skey = stream if stream is not None else eng
            if skey not in self.sems:
                self.sems[skey] = Sem(self.nc, "sem_" + skey)
            sem = self.sems[skey]
        else:
            sem = self.sems[eng]
        inck = 16 if is_dma else 1
        rkeys = [self.key(t) for t in reads]
        wkeys = [self.key(t) for t in writes]
        waits = {}

        def addw(s, v):
            if s is None:
                return
            if eng == "tensor" and s is self.sems["tensor"]:
                return
            k = s
            if waits.get(k, -1) < v:
                waits[k] = v

        for k in rkeys:
            lw = self.last_write.get(k)
            if lw is not None:
                addw(*lw)
        for k in wkeys:
            lw = self.last_write.get(k)
            if lw is not None:
                addw(*lw)
            for s, v in self.readers.get(k, {}).items():
                addw(s, v)
        for s, v in extra:
            addw(s, v)
        final = []
        for s, v in waits.items():
            if self.satisfied[eng].get(s, -1) < v:
                final.append((s, v))
                self.satisfied[eng][s] = v
        final = tuple((s.h, v) for s, v in final)

        if inc:
            sem.n += inck
            v = sem.n

            def run(e, fn=fn, final=final, sh=sem.h, inck=inck):
                for h, val in final:
                    e.wait_ge(h, val)
                fn(e).then_inc(sh, inck)
        else:
            v = None

            def run(e, fn=fn, final=final):
                for h, val in final:
                    e.wait_ge(h, val)
                fn(e)
        self.q[eng].append(run)
        self.log.append((eng, list(rkeys), list(wkeys),
                         [(s.name, v2) for (s, v2) in waits.items()],
                         sem.n + (inck if False else 0) if inc else None))

        if inc:
            for pr, pw in self.pending[eng]:
                for k in pr:
                    self.readers.setdefault(k, {})[sem] = v
                for k in pw:
                    self.last_write[k] = (sem, v)
                    self.readers[k] = {}
            self.pending[eng].clear()
            for k in rkeys:
                self.readers.setdefault(k, {})[sem] = v
            for k in wkeys:
                self.last_write[k] = (sem, v)
                self.readers[k] = {}
        else:
            self.pending[eng].append((rkeys, wkeys))
        return v

    def fence(self, engs=ENGS):
        snap = [(s, s.n) for s in self.sems.values() if s.n > 0]
        for e in engs:
            todo = [(s.h, v) for (s, v) in snap
                    if self.satisfied[e].get(s, -1) < v and s is not self.sems[e]]
            for s, v in snap:
                if self.satisfied[e].get(s, -1) < v and s is not self.sems[e]:
                    self.satisfied[e][s] = v
            if todo:
                def run(eng, todo=tuple(todo)):
                    for h, v in todo:
                        eng.wait_ge(h, v)
                self.q[e].append(run)


def _rope_tables():
    pos = np.arange(S, dtype=np.float64)[:, None]
    freqs = np.exp(-np.arange(0, HD, 2, dtype=np.float64) * (np.log(ROPE_THETA) / HD))
    ang = pos * freqs
    cos = np.concatenate([np.cos(ang), np.cos(ang)], axis=1)
    sin = np.sin(ang)
    sins = np.concatenate([sin, sin], axis=1)
    return cos.T.astype(np.float32), sins.T.astype(np.float32)


def _perm_cols(j):
    idx = []
    for i in range(S // CH):
        cc = i ^ 1 if j == 1 else i
        idx.extend(range(cc * CH, (cc + 1) * CH))
    return np.array(idx)


def _own_cols(j):
    idx = []
    for v in range(4):
        cc = 2 * v + j
        idx.extend(range(cc * CH, (cc + 1) * CH))
    return np.array(idx)


def _mask_tiles(j):
    dk = np.arange(128)[:, None]
    dq = np.arange(CH)[None, :]
    t0 = (dk <= dq)
    t1 = (dk + 128 <= dq)
    tp = np.full((128, CH), bool(j == 1))
    return np.concatenate([t0, t1, tp, tp], axis=1).astype(np.float32).astype(BF16NP)


def _tile_w(wT):
    Din, O = wT.shape
    return np.ascontiguousarray(
        wT.reshape(Din // 128, 128, O // 128, 128).transpose(2, 1, 0, 3))


def build_nc(debug=False):
    nc = bass.Bass("TRN2", target_bir_lowering=False, debug=False)

    dram_in = lambda n, s, dt=F32: nc.dram_tensor(n, s, dt, kind="ExternalInput").ap()
    d_xt = dram_in("xt", [4, 128, NKC, 512])
    d_xo = dram_in("xo", [128, NKC, OWN])
    d_wq = dram_in("wq", [32, 128, NKC, 128])
    d_wk = dram_in("wk", [8, 128, NKC, 128])
    d_wv = dram_in("wv", [8, 128, NKC, 128])
    d_wo = dram_in("wo", [32, 128, NKC, 128])
    d_wb = dram_in("wb", [32, 128, NKC, 128])
    d_wr = dram_in("wr", [128, NKC, 8])
    d_la = dram_in("la", [128, NKC, ER])
    d_lb = dram_in("lb", [ER, D])
    d_bb = dram_in("bb", [128, 32])
    d_br = dram_in("br", [8, 1])
    d_cosk = dram_in("cosk", [128, S], BF16)
    d_sink = dram_in("sink", [128, S], BF16)
    d_cosq = dram_in("cosq", [128, OWN], BF16)
    d_sinq = dram_in("sinq", [128, OWN], BF16)
    d_mask = dram_in("maskt", [128, 4 * CH], BF16)
    d_oneb = dram_in("oneb", [128, 128], BF16)
    d_idb = dram_in("idb", [128, 128], BF16)
    d_idf = dram_in("idf", [128, 128])
    d_e16 = dram_in("e16", [8, 128], BF16)
    d_rk = dram_in("rk", [128, 64])
    d_rotm = dram_in("rotm", [128, 128], BF16)

    d_out = nc.dram_tensor("out", [D, OWN], F32, kind="ExternalOutput").ap()

    skind = "ExternalOutput" if debug else "Internal"
    d_qscr = nc.dram_tensor("qscr", [4, 128, NH * CH], BF16, kind=skind).ap()
    d_oscr = nc.dram_tensor("oscr", [128, NH * OWN], BF16, kind=skind).ap()
    d_x2scr = nc.dram_tensor("x2scr", [128, NKC * OWN], F32, kind=skind).ap()
    if debug:
        d_dxn = nc.dram_tensor("dxn", [2, 128, NKC * 1024], BF16, kind="ExternalOutput").ap()
        d_dk = nc.dram_tensor("dk", [128, KVH * S], BF16, kind="ExternalOutput").ap()
        d_dv = nc.dram_tensor("dv", [128, 16 * 1024], BF16, kind="ExternalOutput").ap()
        d_dw = nc.dram_tensor("dw", [128, 64], F32, kind="ExternalOutput").ap()
        d_dyn = nc.dram_tensor("dyn", [128, NKC * OWN], BF16, kind="ExternalOutput").ap()
        d_dk2 = nc.dram_tensor("dk2", [128, KVH * S], BF16, kind="ExternalOutput").ap()
        d_dv2 = nc.dram_tensor("dv2", [128, 16 * 1024], BF16, kind="ExternalOutput").ap()
        d_dq2 = nc.dram_tensor("dq2", [2, 128, NH * CH], BF16, kind="ExternalOutput").ap()

    sb = nc.alloc_sbuf_tensor
    t_oneb = sb("s_oneb", [128, 128], BF16)
    t_idb = sb("s_idb", [128, 128], BF16)
    t_idf = sb("s_idf", [128, 128], F32)
    t_e16 = sb("s_e16", [8, 128], BF16)
    t_rk = sb("s_rk", [128, 64], F32)
    t_bb = sb("s_bb", [128, 32], F32)
    t_br = sb("s_br", [8, 1], F32)
    t_cosk = sb("s_cosk", [128, S], BF16)
    t_sink = sb("s_sink", [128, S], BF16)
    t_cosq = sb("s_cosq", [128, OWN], BF16)
    t_sinq = sb("s_sinq", [128, OWN], BF16)
    t_mask = sb("s_mask", [128, 4 * CH], BF16)
    t_wrb = sb("s_wrb", [128, NKC * 8], BF16)
    t_rotm = sb("s_rotm", [128, 128], BF16)
    t_t8 = sb("s_t8", [128, 64], F32)
    t_eps = sb("s_eps", [128, 1], F32)
    t_onef = sb("s_onef", [128, 128], F32)
    t_tau = sb("s_tau", [128, 8], F32)
    t_p6t = sb("s_p6t", [128, OWN], F32)

    KB = 1024
    base = (nc.sbuf_base + 31) // 32 * 32
    R1 = base
    R2 = R1 + 64 * KB
    R3 = R2 + 32 * KB
    R4a = R3 + 32 * KB
    R4b = R4a + 16 * KB
    TOP = R4b + 40 * KB
    assert TOP <= nc.sbuf_top - 512, (base, TOP, nc.sbuf_top)

    at = nc.alloc_sbuf_tensor_at
    t_xnh = at("xnh", [128, NKC * 1024], BF16, offset=R1)
    t_obuf = at("obuf", [128, NH * OWN], BF16, offset=R1)
    t_ynb = at("ynb", [128, NKC * OWN], BF16, offset=R1)
    t_kbuf = at("kbuf", [128, KVH * S], BF16, offset=R2)
    t_x2k = [at(f"x2k{i}", [128, OWN], F32, offset=R2 + i * 4 * KB) for i in range(2)]
    t_fo = [at(f"fo{i}", [128, 512], F32, offset=R2 + 8 * KB + i * 2 * KB) for i in range(2)]
    t_outb = [at(f"outb{i}", [128, 512], F32, offset=R2 + 12 * KB + i * 2 * KB) for i in range(2)]
    t_at = [at(f"at{i}", [128, 512], F32, offset=R2 + 16 * KB + i * 2 * KB) for i in range(2)]
    t_x2c = [at(f"x2c{i}", [128, 512], F32, offset=R2 + 20 * KB + i * 2 * KB) for i in range(2)]
    t_xoc = [at(f"xoc{i}", [128, 512], F32, offset=R2 + 24 * KB + i * 2 * KB) for i in range(2)]
    t_vbuf = at("vbuf", [128, 16 * 1024], BF16, offset=R3)
    t_xb2 = [at(f"xb2{i}", [128, 512], BF16, offset=R3 + i * KB) for i in range(2)]
    t_sq2 = [at(f"sq2{i}", [128, 512], BF16, offset=R3 + 2 * KB + i * KB) for i in range(2)]
    t_rstd2 = at("rstd2", [128, OWN], F32, offset=R3 + 4 * KB)
    t_b2 = at("b2", [128, OWN], F32, offset=R3 + 8 * KB)
    t_ha = at("ha", [128, OWN], BF16, offset=R3 + 12 * KB)
    t_g = at("gbuf", [128, OWN], BF16, offset=R3 + 14 * KB)
    t_wB = at("wBb", [128, OWN], BF16, offset=R3 + 16 * KB)
    t_lgT = at("lgT", [8, OWN], F32, offset=R3 + 18 * KB)
    t_lgtok = at("lgtok", [128, 64], F32, offset=R3 + 22 * KB)
    t_wtok = at("wtok", [128, 64], F32, offset=R3 + 22 * KB + 256)
    t_srt = at("srt", [128, 64], F32, offset=R3 + 24 * KB)
    t_srtmp = at("srtmp", [128, 8], F32, offset=R3 + 22 * KB + 512)
    t_wT = at("wTt", [8, OWN], BF16, offset=R3 + 26 * KB)
    t_wbuf = [at(f"wbuf{i}", [128, NKC * 128], BF16, offset=R4a + i * 8 * KB)
              for i in range(2)]
    t_pt = [at(f"pt{i}", [128, 16 * CH], BF16, offset=R4a + i * 8 * KB)
            for i in range(2)]
    t_xc = [at(f"xc{i}", [128, 512], F32, offset=R4b + i * 2 * KB) for i in range(4)]
    t_sqf = [at(f"sqf{i}", [128, 512], F32, offset=R4b + 8 * KB + i * 2 * KB)
             for i in range(2)]
    LNO = R4b + 12 * KB
    t_mS = at("mS", [128, 512], F32, offset=LNO)
    t_qS = at("qS", [128, 512], F32, offset=LNO + 2 * KB)
    t_var = at("var", [128, 512], F32, offset=LNO + 4 * KB)
    t_rstd = at("rstd", [128, 512], F32, offset=LNO + 6 * KB)
    t_bS = at("bS", [128, 512], F32, offset=LNO + 8 * KB)
    QRO = R4b + 22 * KB
    t_qr = [at(f"qr{i}", [128, CH], BF16, offset=QRO + i * 512) for i in range(2)]
    t_q1 = [at(f"q1{i}", [128, CH], BF16, offset=QRO + KB + i * 512) for i in range(2)]
    t_q2 = [at(f"q2{i}", [128, CH], BF16, offset=QRO + 2 * KB + i * 512) for i in range(2)]
    t_qf = [at(f"qf{i}", [128, CH], BF16, offset=QRO + 3 * KB + i * 512) for i in range(2)]
    KRO = R4b + 26 * KB
    t_kr = [at(f"kr{i}", [128, 512], BF16, offset=KRO + i * KB) for i in range(2)]
    t_k1 = [at(f"k1{i}", [128, 512], BF16, offset=KRO + 2 * KB + i * KB) for i in range(2)]
    t_k2 = [at(f"k2{i}", [128, 512], BF16, offset=KRO + 4 * KB + i * KB) for i in range(2)]
    t_vt = [at(f"vt{i}", [128, 512], F32, offset=KRO + 6 * KB + i * 2 * KB) for i in range(2)]
    t_qv = [at(f"qv{i}", [128, NH * CH], BF16, offset=R4b + i * 16 * KB) for i in range(2)]
    ATO = R4b + 32 * KB
    t_ss = [at(f"ss{i}", [128, CH], F32, offset=ATO + i * KB) for i in range(2)]
    t_ot = [at(f"ott{i}", [128, CH], F32, offset=ATO + 2 * KB + i * KB) for i in range(2)]
    t_rs = [at(f"rs{i}", [128, CH], F32, offset=ATO + 4 * KB + i * KB) for i in range(2)]
    t_of = [at(f"of{i}", [128, CH], BF16, offset=ATO + 6 * KB + i * 512) for i in range(2)]
    t_lab = at("lab", [128, NKC * ER], BF16, offset=R4b)
    t_lbb = at("lbb", [ER, D], BF16, offset=R4b + 8 * KB)

    bld = Builder(nc)
    op = bld.op
    sH, sG, sP, sA, sV = (bld.sems[e] for e in ("sync", "gpsimd", "tensor",
                                                "scalar", "vector"))
    sS = sH

    def ld(dst, src, reads=(), inc=True):
        return op("sync", lambda e, o=dst, i=src: e.dma_start(out=o, in_=i),
                  reads=list(reads), writes=[dst.tensor if hasattr(dst, "tensor") else dst],
                  inc=inc)

    def T(x):
        return x.tensor if hasattr(x, "tensor") else x

    xqctr = {"sync": 0, "scalar": 0}

    def dma(eng, dst_ap, src_ap, rd, wr, stream=None):
        return op(eng, lambda e, o=dst_ap, i=src_ap: e.dma_start(out=o, in_=i),
                  reads=rd, writes=wr, is_dma=True, stream=stream)

    def dma_x(eng, dst_ap, src_ap, rd, wr):
        c = xqctr[eng]
        xqctr[eng] = c + 1
        return dma(eng, dst_ap, src_ap, rd, wr, stream=f"x_{eng}_{c % 2}")

    from concourse.hw_specs import get_activation_tables
    _set_id = None
    for idx, (name, funcs) in enumerate(get_activation_tables(nc.m.arch).items()):
        if AF.Exp in funcs and AF.Ln in funcs and AF.Copy in funcs:
            _set_id = idx
            break
    assert _set_id is not None

    def preload(e, _sid=_set_id):
        return e.add_instruction(mybir.InstLoadActFuncSet(
            name=nc.get_next_instruction_name(), act_func_set_id=_sid,
            ins=[], outs=[]))
    op("scalar", preload, inc=False)

    for dst, src in ((t_oneb, d_oneb), (t_idf, d_idf)):
        dma("sync", dst[:], src, rd=[], wr=[dst], stream="const")
    late_consts = ((t_idb, d_idb), (t_e16, d_e16), (t_rk, d_rk), (t_bb, d_bb),
                   (t_br, d_br), (t_cosk, d_cosk), (t_sink, d_sink),
                   (t_cosq, d_cosq), (t_sinq, d_sinq), (t_mask, d_mask),
                   (t_rotm, d_rotm))

    from contextlib import ExitStack

    op("vector", lambda e: e.memset(t_eps[:], EPS), reads=[], writes=[t_eps])
    op("vector", lambda e: e.memset(t_onef[:], 1.0), reads=[], writes=[t_onef])

    with ExitStack() as px:
        ps_sx = px.enter_context(nc.psum_tensor("ps_sx", [128, 512], F32))
        ps_sq = px.enter_context(nc.psum_tensor("ps_sq", [128, 512], F32))
        ps_qa = [px.enter_context(nc.psum_tensor(f"ps_qa{i}", [128, CH], F32))
                 for i in range(2)]
        ps_ka = [px.enter_context(nc.psum_tensor(f"ps_ka{i}", [128, 512], F32))
                 for i in range(2)]
        ps_rot = [px.enter_context(nc.psum_tensor(f"ps_rot{i}", [128, 512], F32))
                  for i in range(2)]

        def q_eng(q, kc):
            if not USE_SCALARQ:
                return "sync"
            if q == "sync":
                return "sync"
            if q == "scalarq":
                return "scalar"
            if q == "split":
                return "sync" if kc % 2 == 0 else "scalar"
            return "scalar" if kc % 2 == 0 else "sync"

        def emit_stats_chunk(half, tg2, kc, q, slots=(0, 1, 2, 3)):
            tg = half * 2 + tg2
            sl, b2 = slots[kc % len(slots)], kc % 2
            dma_x(q_eng(q, kc), t_xc[sl][:], d_xt[tg, :, kc, :],
                  rd=["dram_x"], wr=[t_xc[sl]])
            op("vector",
               lambda e, o=t_sqf[b2], i=t_xc[sl]:
               e.tensor_tensor(o[:], i[:], i[:], ALU.mult),
               reads=[t_xc[sl]], writes=[t_sqf[b2]])
            op("tensor",
               lambda e, r=t_xc[sl], st_=(kc == 0), sp=(kc == NKC - 1):
               e.matmul(ps_sx[:], t_onef[:], r[:], start=st_, stop=sp),
               reads=[t_xc[sl], t_onef], writes=[ps_sx], inc=True)
            op("tensor",
               lambda e, r=t_sqf[b2], st_=(kc == 0), sp=(kc == NKC - 1):
               e.matmul(ps_sq[:], t_onef[:], r[:], start=st_, stop=sp),
               reads=[t_sqf[b2], t_onef], writes=[ps_sq], inc=True)

        def emit_finalize():
            op("scalar", lambda e: e.activation(t_mS[:], ps_sx[:], AF.Copy, scale=INV_D),
               reads=[ps_sx], writes=[t_mS])
            op("scalar", lambda e: e.activation(t_qS[:], ps_sq[:], AF.Copy, scale=INV_D),
               reads=[ps_sq], writes=[t_qS])
            op("vector", lambda e: e.tensor_tensor(t_var[:], t_mS[:], t_mS[:], ALU.mult),
               reads=[t_mS], writes=[t_var])
            op("vector", lambda e: e.tensor_tensor(t_var[:], t_qS[:], t_var[:], ALU.subtract),
               reads=[t_qS, t_var], writes=[t_var])
            op("scalar", lambda e: e.activation(t_rstd[:], t_var[:], AF.Ln, bias=t_eps[:]),
               reads=[t_var, t_eps], writes=[t_rstd])
            op("scalar", lambda e: e.activation(t_rstd[:], t_rstd[:], AF.Exp, scale=-0.5),
               reads=[t_rstd], writes=[t_rstd])
            op("vector", lambda e: e.tensor_tensor(t_bS[:], t_mS[:], t_rstd[:], ALU.mult),
               reads=[t_mS, t_rstd], writes=[t_bS])
            op("vector", lambda e: e.tensor_scalar(t_bS[:], t_bS[:], -1.0, None, ALU.mult),
               reads=[t_bS], writes=[t_bS])

        def emit_normalize_chunk(half, tg2, kc, q="sync", slots=(0, 1, 2, 3)):
            tg = half * 2 + tg2
            sl = slots[kc % len(slots)]
            xnsl = t_xnh[:, kc * 1024 + tg2 * 512:kc * 1024 + tg2 * 512 + 512]
            xkey = f"xn{kc}_{tg2}"
            dma_x(q_eng(q, kc), t_xc[sl][:], d_xt[tg, :, kc, :],
                  rd=["dram_x"], wr=[t_xc[sl]])
            op("vector",
               lambda e, i=t_xc[sl]:
               e.tensor_tensor(t_var[:], i[:], t_rstd[:], ALU.mult),
               reads=[t_xc[sl], t_rstd], writes=[t_var])
            op("vector",
               lambda e, o=xnsl:
               e.tensor_tensor(o, t_var[:], t_bS[:], ALU.add),
               reads=[t_var, t_bS], writes=[xkey])

        def emit_q_group(h, c2, wb):
            pa = ps_qa[c2]
            for kc in range(NKC):
                op("tensor",
                   lambda e, pa=pa, wb=wb, kc=kc, c2=c2,
                   st_=(kc == 0), sp=(kc == NKC - 1):
                   e.matmul(pa[:], wb[:, kc * 128:(kc + 1) * 128],
                            t_xnh[:, kc * 1024 + c2 * 512:
                                  kc * 1024 + c2 * 512 + CH],
                            start=st_, stop=sp),
                   reads=[wb, f"xn{kc}_{c2}"], writes=[pa], inc=(kc == NKC - 1))
            op("scalar",
               lambda e, pa=pa, o=t_qr[c2]: e.activation(o[:], pa[:], AF.Copy),
               reads=[pa], writes=[t_qr[c2]])

        def emit_q_rope(half, h, c2):
            v = half * 2 + c2
            cq = v * CH
            op("tensor",
               lambda e, c2=c2:
               e.matmul(ps_rot[c2][:, 0:CH], t_rotm[:], t_qr[c2][:],
                        start=True, stop=True),
               reads=[t_rotm, t_qr[c2]], writes=[ps_rot[c2]])
            op("scalar",
               lambda e, c2=c2: e.activation(t_q2[c2][:], ps_rot[c2][:, 0:CH],
                                             AF.Copy),
               reads=[ps_rot[c2]], writes=[t_q2[c2]])
            op("vector",
               lambda e, c2=c2, cq=cq:
               e.tensor_tensor(t_q1[c2][:], t_qr[c2][:],
                               t_cosq[:, cq:cq + CH], ALU.mult),
               reads=[t_qr[c2], t_cosq], writes=[t_q1[c2]])
            op("vector",
               lambda e, c2=c2, cq=cq:
               e.tensor_tensor(t_q2[c2][:], t_q2[c2][:],
                               t_sinq[:, cq:cq + CH], ALU.mult),
               reads=[t_q2[c2], t_sinq], writes=[t_q2[c2]])
            op("vector",
               lambda e, c2=c2:
               e.tensor_tensor(t_qf[c2][:], t_q1[c2][:], t_q2[c2][:], ALU.add),
               reads=[t_q1[c2], t_q2[c2]], writes=[t_qf[c2]])
            dma("sync", d_qscr[v, :, h * CH:(h + 1) * CH], t_qf[c2][:],
                rd=[t_qf[c2]], wr=["qscr"], stream="qsp")

        def emit_p2(half, hoist, defer_n=0, pops=2, pop_start=0):
            hoist = list(hoist)
            deferred = []
            for oci in range(48):
                if oci >= pop_start:
                    for _ in range(pops):
                        if hoist:
                            hoist.pop(0)()
                wb = t_wbuf[oci % 2]
                wsrc = (d_wq[oci] if oci < 32 else
                        d_wk[oci - 32] if oci < 40 else d_wv[oci - 40])
                dma("gpsimd", wb[:].rearrange("p (k o) -> p k o", k=NKC), wsrc,
                    rd=["dram_w"], wr=[wb], stream="w")

                if oci < 32:
                    h = oci
                    if oci < defer_n:
                        emit_q_group(h, 0, wb)
                        emit_q_rope(half, h, 0)
                        deferred.append((h, wsrc))
                        continue
                    for c2 in range(2):
                        emit_q_group(h, c2, wb)
                    for c2 in range(2):
                        emit_q_rope(half, h, c2)
                else:
                    kvi = oci - 32 if oci < 40 else oci - 40
                    isK = oci < 40
                    for tgl in range(2):
                        pa = ps_ka[tgl]
                        col0 = tgl * 512
                        for kc in range(NKC):
                            op("tensor",
                               lambda e, pa=pa, wb=wb, kc=kc, col0=col0,
                               st_=(kc == 0), sp=(kc == NKC - 1):
                               e.matmul(pa[:], wb[:, kc * 128:(kc + 1) * 128],
                                        t_xnh[:, kc * 1024 + col0:kc * 1024 + col0 + 512],
                                        start=st_, stop=sp),
                               reads=[wb, f"xn{kc}_{tgl}"], writes=[pa], inc=(kc == NKC - 1))
                        op("scalar",
                           lambda e, pa=pa, o=(t_kr[tgl] if isK else t_vt[tgl]):
                           e.activation(o[:], pa[:], AF.Copy),
                           reads=[pa], writes=[t_kr[tgl] if isK else t_vt[tgl]])
                    for tgl in range(2):
                        gcol = half * 1024 + tgl * 512
                        if isK:
                            op("tensor",
                               lambda e, tgl=tgl:
                               e.matmul(ps_rot[tgl][:], t_rotm[:], t_kr[tgl][:],
                                        start=True, stop=True),
                               reads=[t_rotm, t_kr[tgl]], writes=[ps_rot[tgl]])
                            op("scalar",
                               lambda e, tgl=tgl: e.activation(t_k2[tgl][:], ps_rot[tgl][:],
                                                               AF.Copy),
                               reads=[ps_rot[tgl]], writes=[t_k2[tgl]])
                            op("vector",
                               lambda e, tgl=tgl, gcol=gcol:
                               e.tensor_tensor(t_k1[tgl][:], t_kr[tgl][:],
                                               t_cosk[:, gcol:gcol + 512], ALU.mult),
                               reads=[t_kr[tgl], t_cosk], writes=[t_k1[tgl]])
                            op("vector",
                               lambda e, tgl=tgl, gcol=gcol:
                               e.tensor_tensor(t_k2[tgl][:], t_k2[tgl][:],
                                               t_sink[:, gcol:gcol + 512], ALU.mult),
                               reads=[t_k2[tgl], t_sink], writes=[t_k2[tgl]])
                            op("vector",
                               lambda e, tgl=tgl, kvi=kvi, gcol=gcol:
                               e.tensor_tensor(
                                   t_kbuf[:, kvi * S + gcol:kvi * S + gcol + 512],
                                   t_k1[tgl][:], t_k2[tgl][:], ALU.add),
                               reads=[t_k1[tgl], t_k2[tgl]], writes=[t_kbuf])
                        else:
                            for i4 in range(4):
                                tt = half * 8 + tgl * 4 + i4
                                pv = ps_rot[i4 % 2]
                                op("tensor",
                                   lambda e, tgl=tgl, i4=i4, pv=pv:
                                   e.transpose(pv[:, 0:128], t_vt[tgl][:, i4 * 128:(i4 + 1) * 128],
                                               t_idf[:]),
                                   reads=[t_vt[tgl], t_idf], writes=[pv])
                                op("scalar",
                                   lambda e, pv=pv, tt=tt, kvi=kvi:
                                   e.activation(t_vbuf[:, tt * 1024 + kvi * 128:
                                                       tt * 1024 + kvi * 128 + 128],
                                                pv[:, 0:128], AF.Copy),
                                   reads=[pv], writes=[t_vbuf])
            for j, (h, wsrc) in enumerate(deferred):
                wb = t_wbuf[j % 2]
                dma("gpsimd", wb[:].rearrange("p (k o) -> p k o", k=NKC), wsrc,
                    rd=["dram_w"], wr=[wb], stream="w")
                emit_q_group(h, 1, wb)
                emit_q_rope(half, h, 1)

        if SEQ_LN:
            for half in range(2):
                for tg2 in range(2):
                    for kc in range(NKC):
                        emit_stats_chunk(half, tg2, kc, "split")
                    emit_finalize()
                    if half == 0 and tg2 == 0:
                        for dst, csrc in late_consts:
                            dma("sync", dst[:], csrc, rd=[], wr=[dst], stream="const")
                    for kc in range(NKC):
                        emit_normalize_chunk(half, tg2, kc, "split")
                emit_p2(half, [], defer_n=(DEFER0 if half == 0 else DEFER1))
        else:
            for kc in range(NKC):
                emit_stats_chunk(0, 0, kc, "split")
            emit_finalize()
            for dst, csrc in late_consts:
                dma("sync", dst[:], csrc, rd=[], wr=[dst], stream="const")
            for kc in range(NKC):
                emit_stats_chunk(0, 1, kc, "split", slots=(0, 1))
                emit_normalize_chunk(0, 0, kc, "tilps", slots=(2, 3))
            emit_finalize()
            for kc in range(NKC):
                emit_normalize_chunk(0, 1, kc, "split")
            hoist = [(lambda kc=kc: emit_stats_chunk(1, 0, kc, "split")) for kc in range(NKC)]
            emit_p2(0, hoist, defer_n=DEFER0, pops=4, pop_start=40)
            emit_finalize()
            for kc in range(NKC):
                emit_stats_chunk(1, 1, kc, "split", slots=(0, 1))
                emit_normalize_chunk(1, 0, kc, "tilps", slots=(2, 3))
            emit_finalize()
            for kc in range(NKC):
                emit_normalize_chunk(1, 1, kc, "split")
            emit_p2(1, [], defer_n=DEFER1)

    if debug:
        dma("sync", d_dxn[1], t_xnh[:], rd=[t_xnh], wr=["ddxn"], stream="dbg")
        dma("sync", d_dk, t_kbuf[:], rd=[t_kbuf], wr=["ddk"], stream="dbg")
        dma("sync", d_dv, t_vbuf[:], rd=[t_vbuf], wr=["ddv"], stream="dbg")

    bld.fence()
    with ExitStack() as px:
        ps_st = [px.enter_context(nc.psum_tensor(f"ps_st{i}", [128, 512], F32))
                 for i in range(4)]
        ps_sm = [px.enter_context(nc.psum_tensor(f"ps_sm{i}", [128, CH], F32))
                 for i in range(2)]
        ps_ot = [px.enter_context(nc.psum_tensor(f"ps_ot{i}", [128, CH], F32))
                 for i in range(2)]

        LEAD = 3
        pairs = []
        for v in range(4):
            for h in range(NH):
                G = 2 * v + 2
                for g in range(G):
                    pairs.append((v, h, g, G))
        qv_of = {}
        for v in range(4):
            qv_of[v] = t_qv[v % 2]

        hv_of = {}
        hv = 0
        for v in range(4):
            for h in range(NH):
                hv_of[(v, h)] = hv
                hv += 1

        def emit_st(idx):
            v, h, g, G = pairs[idx]
            qv = qv_of[v]
            kvh = h // GROUPS
            pst = ps_st[idx % 4]
            for i2 in range(2):
                kb = 2 * g + i2
                op("tensor",
                   lambda e, pst=pst, i2=i2, kb=kb, kvh=kvh, h=h, qv=qv:
                   e.matmul(pst[:, i2 * CH:(i2 + 1) * CH],
                            t_kbuf[:, kvh * S + kb * 128:kvh * S + kb * 128 + 128],
                            qv[:, h * CH:(h + 1) * CH],
                            start=True, stop=True),
                   reads=[t_kbuf, qv], writes=[pst], inc=(i2 == 1))

        def emit_consume(idx):
            v, h, g, G = pairs[idx]
            kvh = h // GROUPS
            hv = hv_of[(v, h)]
            ptb = t_pt[hv % 2]
            pss, pso = ps_sm[hv % 2], ps_ot[hv % 2]
            nkb = 2 * G
            pst = ps_st[idx % 4]
            pk = lambda kb, par=hv % 2: f"ptsl{par}_{kb}"
            op("scalar",
               lambda e, pst=pst, ptb=ptb, g=g:
               e.activation(ptb[:, g * 512:(g + 1) * 512], pst[:], AF.Exp,
                            scale=SCALE),
               reads=[pst], writes=[pk(2 * g), pk(2 * g + 1)])
            for i2 in range(2):
                kb = 2 * g + i2
                mi = kb - 4 * v
                if mi >= 0:
                    op("vector",
                       lambda e, ptb=ptb, kb=kb, mi=mi:
                       e.tensor_tensor(ptb[:, kb * CH:(kb + 1) * CH],
                                       ptb[:, kb * CH:(kb + 1) * CH],
                                       t_mask[:, mi * CH:(mi + 1) * CH], ALU.mult),
                       reads=[pk(kb), t_mask], writes=[pk(kb)])
                op("tensor",
                   lambda e, pss=pss, ptb=ptb, kb=kb,
                   st_=(kb == 0), sp=(kb == nkb - 1):
                   e.matmul(pss[:], t_oneb[:], ptb[:, kb * CH:(kb + 1) * CH],
                            start=st_, stop=sp),
                   reads=[t_oneb, pk(kb)], writes=[pss], inc=(kb == nkb - 1))
                op("tensor",
                   lambda e, pso=pso, ptb=ptb, kb=kb, kvh=kvh,
                   st_=(kb == 0), sp=(kb == nkb - 1):
                   e.matmul(pso[:],
                            t_vbuf[:, kb * 1024 + kvh * 128:kb * 1024 + kvh * 128 + 128],
                            ptb[:, kb * CH:(kb + 1) * CH],
                            start=st_, stop=sp),
                   reads=[t_vbuf, pk(kb)], writes=[pso], inc=(kb == nkb - 1))
            if g == G - 1:
                def tail(pss=pss, pso=pso, hv=hv, h=h, v=v):
                    ssb, otb, rsb = t_ss[hv % 2], t_ot[hv % 2], t_rs[hv % 2]
                    op("scalar", lambda e: e.activation(ssb[:], pss[:], AF.Ln),
                       reads=[pss], writes=[ssb])
                    op("scalar", lambda e: e.activation(rsb[:], ssb[:], AF.Exp,
                                                        scale=-1.0),
                       reads=[ssb], writes=[rsb])
                    op("scalar", lambda e: e.activation(otb[:], pso[:], AF.Copy),
                       reads=[pso], writes=[otb])
                    op("vector", lambda e:
                       e.tensor_tensor(t_obuf[:, h * OWN + v * CH:h * OWN + v * CH + CH],
                                       otb[:], rsb[:], ALU.mult),
                       reads=[otb, rsb], writes=[t_obuf])
                tail_q.append((idx, tail))

        NP = len(pairs)
        start_idx = {}
        acc = 0
        for v in range(4):
            start_idx[v] = acc
            acc += NH * (2 * v + 2)
        triggers = {start_idx[v - 1] + NH * (2 * (v - 1) + 2) // 2: v
                    for v in range(1, 4)}
        dma("sync", qv_of[0][:], d_qscr[0], rd=["qscr"], wr=[qv_of[0]], stream="qv")
        tail_q = []
        for idx in range(min(LEAD, NP)):
            emit_st(idx)
        for idx in range(NP):
            v_pre = triggers.get(idx)
            if v_pre is not None:
                dma("sync", qv_of[v_pre][:], d_qscr[v_pre],
                    rd=["qscr"], wr=[qv_of[v_pre]], stream="qv")
            if idx + LEAD < NP:
                emit_st(idx + LEAD)
            emit_consume(idx)
            while tail_q and tail_q[0][0] <= idx - 2:
                tail_q.pop(0)[1]()
        while tail_q:
            tail_q.pop(0)[1]()
    bld.fence()
    with ExitStack() as px:
        ps_wo = [px.enter_context(nc.psum_tensor(f"ps_wo{i}", [128, 512], F32))
                 for i in range(2)]
        ps_s2x = [px.enter_context(nc.psum_tensor(f"ps_s2x{i}", [128, 512], F32))
                  for i in range(2)]
        ps_s2q = [px.enter_context(nc.psum_tensor(f"ps_s2q{i}", [128, 512], F32))
                  for i in range(2)]

        stat_mm_pending = []
        for oc in range(32):
            wb = t_wbuf[oc % 2]
            dma("gpsimd", wb[:].rearrange("p (k o) -> p k o", k=NKC), d_wo[oc], rd=["dram_w"], wr=[wb], stream="w")
            for tg in range(2):
                pw = ps_wo[tg]
                for kc in range(NKC):
                    op("tensor",
                       lambda e, pw=pw, wb=wb, kc=kc, tg=tg,
                       st_=(kc == 0), sp=(kc == NKC - 1):
                       e.matmul(pw[:], wb[:, kc * 128:(kc + 1) * 128],
                                t_obuf[:, kc * 1024 + tg * 512:kc * 1024 + tg * 512 + 512],
                                start=st_, stop=sp),
                       reads=[wb, t_obuf], writes=[pw], inc=(kc == NKC - 1))
                for th in stat_mm_pending:
                    th()
                stat_mm_pending = []
                atb, x2b, xob = t_at[tg], t_x2c[tg], t_xoc[tg]
                op("scalar", lambda e, pw=pw, atb=atb: e.activation(atb[:], pw[:], AF.Copy),
                   reads=[pw], writes=[atb])
                dma("sync", xob[:], d_xo[:, oc, tg * 512:(tg + 1) * 512],
                    rd=["dram_xo"], wr=[xob], stream="xo")
                op("vector", lambda e, atb=atb, xob=xob, x2b=x2b:
                   e.tensor_tensor(x2b[:], atb[:], xob[:], ALU.add),
                   reads=[atb, xob], writes=[x2b])
                dma("sync", d_x2scr[:, oc * OWN + tg * 512:oc * OWN + tg * 512 + 512],
                    x2b[:], rd=[x2b], wr=["x2scr"], stream="x2sp")
                xb2, sq2 = t_xb2[tg], t_sq2[tg]
                op("scalar", lambda e, x2b=x2b, xb2=xb2: e.activation(xb2[:], x2b[:], AF.Copy),
                   reads=[x2b], writes=[xb2])
                op("vector", lambda e, xb2=xb2, sq2=sq2:
                   e.tensor_tensor(sq2[:], xb2[:], xb2[:], ALU.mult),
                   reads=[xb2], writes=[sq2])

                def mk_stats(xb2=xb2, sq2=sq2, tg=tg, oc=oc):
                    def th():
                        op("tensor",
                           lambda e, st_=(oc == 0), sp=(oc == 31):
                           e.matmul(ps_s2x[tg][:], t_oneb[:], xb2[:], start=st_, stop=sp),
                           reads=[t_oneb, xb2], writes=[ps_s2x[tg]], inc=(oc == 31))
                        op("tensor",
                           lambda e, st_=(oc == 0), sp=(oc == 31):
                           e.matmul(ps_s2q[tg][:], t_oneb[:], sq2[:], start=st_, stop=sp),
                           reads=[t_oneb, sq2], writes=[ps_s2q[tg]], inc=(oc == 31))
                    return th
                stat_mm_pending.append(mk_stats())
        for th in stat_mm_pending:
            th()

        for tg in range(2):
            rsl = t_rstd2[:, tg * 512:(tg + 1) * 512]
            bsl = t_b2[:, tg * 512:(tg + 1) * 512]
            op("scalar", lambda e, tg=tg: e.activation(t_mS[:], ps_s2x[tg][:], AF.Copy,
                                                       scale=INV_D),
               reads=[ps_s2x[tg]], writes=[t_mS])
            op("scalar", lambda e, tg=tg: e.activation(t_qS[:], ps_s2q[tg][:], AF.Copy,
                                                       scale=INV_D),
               reads=[ps_s2q[tg]], writes=[t_qS])
            op("vector", lambda e: e.tensor_tensor(t_var[:], t_mS[:], t_mS[:], ALU.mult),
               reads=[t_mS], writes=[t_var])
            op("vector", lambda e: e.tensor_tensor(t_var[:], t_qS[:], t_var[:], ALU.subtract),
               reads=[t_qS, t_var], writes=[t_var])
            op("scalar", lambda e: e.activation(t_rstd[:], t_var[:], AF.Ln, bias=t_eps[:]),
               reads=[t_var, t_eps], writes=[t_rstd])
            op("scalar", lambda e, rsl=rsl: e.activation(rsl, t_rstd[:], AF.Exp, scale=-0.5),
               reads=[t_rstd], writes=[t_rstd2])
            op("vector", lambda e, rsl=rsl: e.tensor_tensor(t_bS[:], t_mS[:], rsl, ALU.mult),
               reads=[t_mS, t_rstd2], writes=[t_bS])
            op("vector", lambda e, bsl=bsl: e.tensor_scalar(bsl, t_bS[:], -1.0, None, ALU.mult),
               reads=[t_bS], writes=[t_b2])


    bld.fence()
    with ExitStack() as px:
        ps_lg = [px.enter_context(nc.psum_tensor(f"ps_lg{i}", [8, 512], F32))
                 for i in range(2)]
        ps_ha = [px.enter_context(nc.psum_tensor(f"ps_ha{i}", [128, 512], F32))
                 for i in range(2)]
        ps_tp8 = [px.enter_context(nc.psum_tensor(f"ps_tp8{i}", [128, 8], F32))
                  for i in range(2)]
        ps_tpw = px.enter_context(nc.psum_tensor("ps_tpw", [8, 128], F32))
        ps_wb2 = [px.enter_context(nc.psum_tensor("ps_wb2x", [128, 512], F32))] * 2

        dma("gpsimd", t_wrb[:].rearrange("p (k e) -> p k e", k=NKC), d_wr, stream="w6r",
            rd=["dram_w"], wr=[t_wrb])
        dma("gpsimd", t_lab[:].rearrange("p (k r) -> p k r", k=NKC), d_la, stream="w6a",
            rd=["dram_w"], wr=[t_lab])
        dma("gpsimd", t_lbb[:], d_lb, rd=["dram_w"], wr=[t_lbb], stream="w6b")

        for kc in range(NKC):
            xk = t_x2k[kc % 2]
            dma("sync", xk[:], d_x2scr[:, kc * OWN:(kc + 1) * OWN], rd=["x2scr"], wr=[xk], stream="x2ld")
            op("vector", lambda e, xk=xk: e.tensor_tensor(t_p6t[:], xk[:], t_rstd2[:], ALU.mult),
               reads=[xk, t_rstd2], writes=[t_p6t])
            op("vector", lambda e, kc=kc:
               e.tensor_tensor(t_ynb[:, kc * OWN:(kc + 1) * OWN], t_p6t[:], t_b2[:], ALU.add),
               reads=[t_p6t, t_b2], writes=[t_ynb])
            for tg in range(2):
                ysl = t_ynb[:, kc * OWN + tg * 512:kc * OWN + tg * 512 + 512]
                op("tensor",
                   lambda e, kc=kc, tg=tg, ysl=ysl, st_=(kc == 0), sp=(kc == NKC - 1):
                   e.matmul(ps_lg[tg][:], t_wrb[:, kc * 8:(kc + 1) * 8], ysl,
                            start=st_, stop=sp),
                   reads=[t_wrb, t_ynb], writes=[ps_lg[tg]], inc=(kc == NKC - 1))
        for tg in range(2):
            op("scalar", lambda e, tg=tg:
               e.activation(t_lgT[:, tg * 512:(tg + 1) * 512], ps_lg[tg][:], AF.Copy),
               reads=[ps_lg[tg]], writes=[t_lgT])
            op("vector", lambda e, tg=tg:
               e.tensor_scalar(t_lgT[:, tg * 512:(tg + 1) * 512],
                               t_lgT[:, tg * 512:(tg + 1) * 512], t_br[:], None, ALU.add),
               reads=[t_lgT, t_br], writes=[t_lgT])
        for ti in range(8):
            pvx = ps_tp8[ti % 2]
            op("tensor", lambda e, ti=ti, pvx=pvx:
               e.transpose(pvx[:], t_lgT[0:8, ti * 128:(ti + 1) * 128], t_idf[0:8, 0:8]),
               reads=[t_lgT, t_idf], writes=[pvx])
            op("scalar", lambda e, ti=ti, pvx=pvx:
               e.activation(t_lgtok[:, ti * 8:(ti + 1) * 8], pvx[:], AF.Copy),
               reads=[pvx], writes=[t_lgtok])
        op("vector", lambda e: e.tensor_copy(t_srt[:], t_lgtok[:]),
           reads=[t_lgtok], writes=[t_srt])

        def s_sl(i):
            return t_srt[:].rearrange("p (t e) -> p t e", e=8)[:, :, i]
        for (i, j) in SORT8:
            op("vector", lambda e, i=i, j=j:
               e.tensor_tensor(t_srtmp[:], s_sl(i), s_sl(j), ALU.min),
               reads=[t_srt], writes=[t_srtmp])
            op("vector", lambda e, i=i, j=j:
               e.tensor_tensor(s_sl(i), s_sl(i), s_sl(j), ALU.max),
               reads=[t_srt], writes=[t_srt])
            op("vector", lambda e, j=j:
               e.tensor_copy(s_sl(j), t_srtmp[:]),
               reads=[t_srtmp], writes=[t_srt])
        for k in range(1, 8):
            op("vector", lambda e, k=k:
               e.tensor_tensor(s_sl(k), s_sl(k), s_sl(k - 1), ALU.add),
               reads=[t_srt], writes=[t_srt])
        op("vector", lambda e:
           e.tensor_scalar(t_t8[:], t_srt[:], -1.0, None, ALU.add),
           reads=[t_srt], writes=[t_t8])
        op("vector", lambda e:
           e.tensor_tensor(t_t8[:], t_t8[:], t_rk[:], ALU.mult),
           reads=[t_t8, t_rk], writes=[t_t8])
        op("vector", lambda e:
           e.tensor_reduce(t_tau[:], t_t8[:].rearrange("p (t e) -> p t e", e=8),
                           AX.X, ALU.max),
           reads=[t_t8], writes=[t_tau])
        for ti in range(8):
            op("vector", lambda e, ti=ti:
               e.tensor_scalar(t_wtok[:, ti * 8:(ti + 1) * 8], t_lgtok[:, ti * 8:(ti + 1) * 8],
                               t_tau[:, ti:ti + 1], 0.0, ALU.subtract, ALU.max),
               reads=[t_lgtok, t_tau], writes=[t_wtok])
        for tg in range(2):
            for kc in range(NKC):
                ysl = t_ynb[:, kc * OWN + tg * 512:kc * OWN + tg * 512 + 512]
                op("tensor",
                   lambda e, kc=kc, tg=tg, ysl=ysl, st_=(kc == 0), sp=(kc == NKC - 1):
                   e.matmul(ps_ha[tg][:], t_lab[:, kc * ER:(kc + 1) * ER], ysl,
                            start=st_, stop=sp),
                   reads=[t_lab, t_ynb], writes=[ps_ha[tg]], inc=(kc == NKC - 1))
            op("scalar", lambda e, tg=tg:
               e.activation(t_ha[:, tg * 512:(tg + 1) * 512], ps_ha[tg][:], AF.Copy),
               reads=[ps_ha[tg]], writes=[t_ha])
        for ti in range(8):
            op("tensor", lambda e, ti=ti:
               e.transpose(ps_tpw[:], t_wtok[:, ti * 8:(ti + 1) * 8], t_idf[:]),
               reads=[t_wtok, t_idf], writes=[ps_tpw])
            op("scalar", lambda e, ti=ti:
               e.activation(t_wT[:, ti * 128:(ti + 1) * 128], ps_tpw[:], AF.Copy),
               reads=[ps_tpw], writes=[t_wT])
        for tg in range(2):
            op("tensor", lambda e, tg=tg:
               e.matmul(ps_wb2[tg][:], t_e16[:], t_wT[:, tg * 512:(tg + 1) * 512],
                        start=True, stop=True),
               reads=[t_e16, t_wT], writes=[ps_wb2[tg]])
            op("scalar", lambda e, tg=tg:
               e.activation(t_wB[:, tg * 512:(tg + 1) * 512], ps_wb2[tg][:], AF.Copy),
               reads=[ps_wb2[tg]], writes=[t_wB])
        op("vector", lambda e: e.tensor_tensor(t_g[:], t_ha[:], t_wB[:], ALU.mult),
           reads=[t_ha, t_wB], writes=[t_g])
        if debug:
            dma("sync", d_dyn, t_ynb[:], rd=[t_ynb], wr=["ddyn"], stream="dbg")
            dma("sync", d_dw, t_wtok[:], rd=[t_wtok], wr=["ddw"], stream="dbg")

    bld.fence(("tensor", "scalar"))
    with ExitStack() as px:
        ps_fb = [px.enter_context(nc.psum_tensor(f"ps_fb{i}", [128, 512], F32))
                 for i in range(2)]
        for oc in range(32):
            wb = t_wbuf[oc % 2]
            dma("gpsimd", wb[:].rearrange("p (k o) -> p k o", k=NKC), d_wb[oc], rd=["dram_w"], wr=[wb], stream="w")
            for tg in range(2):
                pf = ps_fb[tg]
                for kc in range(NKC):
                    op("tensor",
                       lambda e, pf=pf, wb=wb, kc=kc, tg=tg, st_=(kc == 0):
                       e.matmul(pf[:], wb[:, kc * 128:(kc + 1) * 128],
                                t_ynb[:, kc * OWN + tg * 512:kc * OWN + tg * 512 + 512],
                                start=st_, stop=False),
                       reads=[wb, t_ynb], writes=[pf], inc=False)
                op("tensor",
                   lambda e, pf=pf, oc=oc, tg=tg:
                   e.matmul(pf[:], t_lbb[:, oc * 128:(oc + 1) * 128],
                            t_g[:, tg * 512:(tg + 1) * 512], start=False, stop=True),
                   reads=[t_lbb, t_g], writes=[pf], inc=True)
                fob, x2b, ob = t_fo[tg], t_x2c[tg], t_outb[tg]
                op("scalar", lambda e, pf=pf, fob=fob:
                   e.activation(fob[:], pf[:], AF.Copy),
                   reads=[pf], writes=[fob])
                dma("sync", x2b[:],
                    d_x2scr[:, oc * OWN + tg * 512:oc * OWN + tg * 512 + 512],
                    rd=["x2scr"], wr=[x2b], stream="x2lds")
                op("vector", lambda e, fob=fob, x2b=x2b, ob=ob, oc=oc:
                   e.scalar_tensor_tensor(ob[:], fob[:], t_bb[:, oc:oc + 1], x2b[:],
                                          ALU.add, ALU.add),
                   reads=[fob, x2b, t_bb], writes=[ob])
                dma("sync", d_out[oc * 128:(oc + 1) * 128, tg * 512:(tg + 1) * 512],
                    ob[:], rd=[ob], wr=["dout"], stream="out")

    finals = [(s.h, s.n) for s in bld.sems.values() if s.n > 0]

    def _final_waits(e, finals=tuple(finals)):
        for h, v in finals:
            e.wait_ge(h, v)
    bld.q["sync"].append(_final_waits)

    with nc.Block() as block:
        @block.sync
        def _(eng):
            for f in bld.q["sync"]:
                f(eng)

        @block.gpsimd
        def _(eng):
            for f in bld.q["gpsimd"]:
                f(eng)

        @block.tensor
        def _(eng):
            for f in bld.q["tensor"]:
                f(eng)

        @block.scalar
        def _(eng):
            for f in bld.q["scalar"]:
                f(eng)

        @block.vector
        def _(eng):
            for f in bld.q["vector"]:
                f(eng)

    return nc


_NC_CACHE = {}


def _get_nc(debug=False):
    if debug not in _NC_CACHE:
        _NC_CACHE[debug] = build_nc(debug)
    return _NC_CACHE[debug]


def make_in_maps(inputs):
    x = np.asarray(inputs["L0_attn_hidden_states"])
    wq = np.asarray(inputs["L0_attn_Wq"])
    wk = np.asarray(inputs["L0_attn_Wk"])
    wv = np.asarray(inputs["L0_attn_Wv"])
    wo = np.asarray(inputs["L0_attn_Wo"])
    wbase = np.asarray(inputs["L0_ffn_W_base"])
    bbase = np.asarray(inputs["L0_ffn_b_base"])
    wrout = np.asarray(inputs["L0_ffn_W_router"])
    brout = np.asarray(inputs["L0_ffn_b_router"])
    la = np.asarray(inputs["L0_ffn_lora_A"])
    lb = np.asarray(inputs["L0_ffn_lora_B"])

    wq_t = _tile_w(np.ascontiguousarray(wq.T))
    wk_t = _tile_w(np.ascontiguousarray(wk.T))
    wv_t = _tile_w(np.ascontiguousarray(wv.T))
    wo_t = _tile_w(np.ascontiguousarray(wo.T))
    wb_t = _tile_w(np.ascontiguousarray(wbase.T))
    wr_t = np.ascontiguousarray(wrout.T.reshape(NKC, 128, 8).transpose(1, 0, 2))
    la_t = np.ascontiguousarray(
        la.transpose(1, 0, 2).reshape(D, ER).reshape(NKC, 128, ER).transpose(1, 0, 2))
    lb_t = np.ascontiguousarray(lb.reshape(ER, D))
    bb_t = np.ascontiguousarray(bbase.reshape(32, 128).T)
    br_t = np.ascontiguousarray(brout.reshape(8, 1))
    cosT, sinT = _rope_tables()
    oneb = np.ones((128, 128), np.float32).astype(BF16NP)
    idb = np.eye(128, dtype=np.float32).astype(BF16NP)
    idf = np.eye(128, dtype=np.float32)
    e16 = np.zeros((8, 128), np.float32)
    for e in range(E_):
        e16[e, e * R_:(e + 1) * R_] = 1.0
    e16 = e16.astype(BF16NP)
    rk = np.broadcast_to(np.tile(1.0 / np.arange(1, 9, dtype=np.float32), 8), (128, 64)).copy()
    rotm = np.zeros((128, 128), np.float32)
    for dd in range(64):
        rotm[dd + 64, dd] = -1.0
        rotm[dd, dd + 64] = 1.0
    rotm = rotm.astype(BF16NP)

    in_maps = []
    for c in range(8):
        b, j = c // 2, c % 2
        pc = _perm_cols(j)
        oc_ = _own_cols(j)
        xp = x[b][pc].T
        xt = np.ascontiguousarray(
            xp.reshape(NKC, 128, 4, 512).transpose(2, 1, 0, 3))
        xo = np.ascontiguousarray(
            x[b][oc_].T.reshape(NKC, 128, OWN).transpose(1, 0, 2))
        cosk = np.ascontiguousarray(cosT[:, pc]).astype(BF16NP)
        sink = np.ascontiguousarray(sinT[:, pc]).astype(BF16NP)
        cosq = np.ascontiguousarray(cosT[:, oc_]).astype(BF16NP)
        sinq = np.ascontiguousarray(sinT[:, oc_]).astype(BF16NP)
        in_maps.append(dict(
            xt=xt, xo=xo, wq=wq_t, wk=wk_t, wv=wv_t, wo=wo_t, wb=wb_t,
            wr=wr_t, la=la_t, lb=lb_t, bb=bb_t, br=br_t,
            cosk=cosk, sink=sink, cosq=cosq, sinq=sinq,
            maskt=_mask_tiles(j), oneb=oneb, idb=idb, idf=idf, e16=e16, rk=rk,
            rotm=rotm))
    return in_maps


def kernel(**inputs):
    in_maps = make_in_maps(inputs)
    nc = _get_nc(False)
    res = run_bass_kernel_spmd(nc, in_maps, core_ids=list(range(8))).results
    out = np.empty((B, S, D), np.float32)
    for c in range(8):
        b, j = c // 2, c % 2
        out[b, _own_cols(j), :] = res[c]["out"].T
    return out


# version 23
# speedup vs baseline: 1.0560x; 1.0560x over previous
import numpy as np
import ml_dtypes
import concourse.bass as bass
import concourse.mybir as mybir
from concourse.bass_utils import run_bass_kernel_spmd

F32 = mybir.dt.float32
BF16 = mybir.dt.bfloat16
AF = mybir.ActivationFunctionType
ALU = mybir.AluOpType
AX = mybir.AxisListType
BF16NP = ml_dtypes.bfloat16

B, S, D = 4, 2048, 4096
NH, KVH, HD = 32, 8, 128
GROUPS = NH // KVH
NKC = D // 128
CH = 256
OWN = S // 2
E_, R_ = 8, 16
ER = E_ * R_
SCALE = float(1.0 / np.sqrt(HD))
EPS = 1e-5
ROPE_THETA = 10000.0
INV_D = 1.0 / D

ENGS = ("sync", "gpsimd", "tensor", "scalar", "vector")
P3_PIPE = True
USE_SCALARQ = True
DEFER0 = 3
DEFER1 = 3
SEQ_LN = True

SORT8 = [(0, 1), (2, 3), (4, 5), (6, 7), (0, 2), (1, 3), (4, 6), (5, 7),
         (1, 2), (5, 6), (0, 4), (1, 5), (2, 6), (3, 7), (2, 4), (3, 5),
         (1, 2), (3, 4), (5, 6)]


class Sem:
    def __init__(self, nc, name):
        self.h = nc.alloc_semaphore(name)
        self.name = name
        self.n = 0


class Builder:

    def __init__(self, nc):
        self.nc = nc
        self.q = {e: [] for e in ENGS}
        self.sems = {}
        for e in ENGS:
            self.sems[e] = Sem(nc, "sem_" + e)
        self.sems["scalarq"] = Sem(nc, "sem_scalarq")
        self.esem = Sem(nc, "sem_end")
        self.last_write = {}
        self.readers = {}
        self.satisfied = {e: {} for e in ENGS}
        self.pending = {e: [] for e in ENGS}
        self.log = []

    @staticmethod
    def key(t):
        return t if isinstance(t, str) else t.name

    def op(self, eng, fn, reads=(), writes=(), inc=True, extra=(), is_dma=None,
           stream=None):
        if is_dma is None:
            is_dma = eng in ("sync", "gpsimd")
        if is_dma:
            skey = stream if stream is not None else eng
            if skey not in self.sems:
                self.sems[skey] = Sem(self.nc, "sem_" + skey)
            sem = self.sems[skey]
        else:
            sem = self.sems[eng]
        inck = 16 if is_dma else 1
        rkeys = [self.key(t) for t in reads]
        wkeys = [self.key(t) for t in writes]
        waits = {}

        def addw(s, v):
            if s is None:
                return
            if eng == "tensor" and s is self.sems["tensor"]:
                return
            k = s
            if waits.get(k, -1) < v:
                waits[k] = v

        for k in rkeys:
            lw = self.last_write.get(k)
            if lw is not None:
                addw(*lw)
        for k in wkeys:
            lw = self.last_write.get(k)
            if lw is not None:
                addw(*lw)
            for s, v in self.readers.get(k, {}).items():
                addw(s, v)
        for s, v in extra:
            addw(s, v)
        final = []
        for s, v in waits.items():
            if self.satisfied[eng].get(s, -1) < v:
                final.append((s, v))
                self.satisfied[eng][s] = v
        final = tuple((s.h, v) for s, v in final)

        if inc:
            sem.n += inck
            v = sem.n

            def run(e, fn=fn, final=final, sh=sem.h, inck=inck):
                for h, val in final:
                    e.wait_ge(h, val)
                fn(e).then_inc(sh, inck)
        else:
            v = None

            def run(e, fn=fn, final=final):
                for h, val in final:
                    e.wait_ge(h, val)
                fn(e)
        self.q[eng].append(run)
        self.log.append((eng, list(rkeys), list(wkeys),
                         [(s.name, v2) for (s, v2) in waits.items()],
                         sem.n + (inck if False else 0) if inc else None))

        if inc:
            for pr, pw in self.pending[eng]:
                for k in pr:
                    self.readers.setdefault(k, {})[sem] = v
                for k in pw:
                    self.last_write[k] = (sem, v)
                    self.readers[k] = {}
            self.pending[eng].clear()
            for k in rkeys:
                self.readers.setdefault(k, {})[sem] = v
            for k in wkeys:
                self.last_write[k] = (sem, v)
                self.readers[k] = {}
        else:
            self.pending[eng].append((rkeys, wkeys))
        return v

    def fence(self, engs=ENGS):
        snap = [(s, s.n) for s in self.sems.values() if s.n > 0]
        for e in engs:
            todo = [(s.h, v) for (s, v) in snap
                    if self.satisfied[e].get(s, -1) < v and s is not self.sems[e]]
            for s, v in snap:
                if self.satisfied[e].get(s, -1) < v and s is not self.sems[e]:
                    self.satisfied[e][s] = v
            if todo:
                def run(eng, todo=tuple(todo)):
                    for h, v in todo:
                        eng.wait_ge(h, v)
                self.q[e].append(run)


def _rope_tables():
    pos = np.arange(S, dtype=np.float64)[:, None]
    freqs = np.exp(-np.arange(0, HD, 2, dtype=np.float64) * (np.log(ROPE_THETA) / HD))
    ang = pos * freqs
    cos = np.concatenate([np.cos(ang), np.cos(ang)], axis=1)
    sin = np.sin(ang)
    sins = np.concatenate([sin, sin], axis=1)
    return cos.T.astype(np.float32), sins.T.astype(np.float32)


def _perm_cols(j):
    idx = []
    for i in range(S // CH):
        cc = i ^ 1 if j == 1 else i
        idx.extend(range(cc * CH, (cc + 1) * CH))
    return np.array(idx)


def _own_cols(j):
    idx = []
    for v in range(4):
        cc = 2 * v + j
        idx.extend(range(cc * CH, (cc + 1) * CH))
    return np.array(idx)


def _mask_tiles(j):
    dk = np.arange(128)[:, None]
    dq = np.arange(CH)[None, :]
    t0 = (dk <= dq)
    t1 = (dk + 128 <= dq)
    tp = np.full((128, CH), bool(j == 1))
    return np.concatenate([t0, t1, tp, tp], axis=1).astype(np.float32).astype(BF16NP)


def _tile_w(wT):
    Din, O = wT.shape
    return np.ascontiguousarray(
        wT.reshape(Din // 128, 128, O // 128, 128).transpose(2, 1, 0, 3))


def build_nc(debug=False):
    nc = bass.Bass("TRN2", target_bir_lowering=False, debug=False)

    dram_in = lambda n, s, dt=F32: nc.dram_tensor(n, s, dt, kind="ExternalInput").ap()
    d_xt = dram_in("xt", [4, 128, NKC, 512])
    d_xo = dram_in("xo", [128, NKC, OWN])
    d_wq = dram_in("wq", [32, 128, NKC, 128])
    d_wk = dram_in("wk", [8, 128, NKC, 128])
    d_wv = dram_in("wv", [8, 128, NKC, 128])
    d_wo = dram_in("wo", [32, 128, NKC, 128])
    d_wb = dram_in("wb", [32, 128, NKC, 128])
    d_wr = dram_in("wr", [128, NKC, 8])
    d_la = dram_in("la", [128, NKC, ER])
    d_lb = dram_in("lb", [ER, D])
    d_bb = dram_in("bb", [128, 32])
    d_br = dram_in("br", [8, 1])
    d_cosk = dram_in("cosk", [128, S], BF16)
    d_sink = dram_in("sink", [128, S], BF16)
    d_cosq = dram_in("cosq", [128, OWN], BF16)
    d_sinq = dram_in("sinq", [128, OWN], BF16)
    d_mask = dram_in("maskt", [128, 4 * CH], BF16)
    d_oneb = dram_in("oneb", [128, 128], BF16)
    d_idb = dram_in("idb", [128, 128], BF16)
    d_idf = dram_in("idf", [128, 128])
    d_e16 = dram_in("e16", [8, 128], BF16)
    d_rk = dram_in("rk", [128, 64])
    d_rotm = dram_in("rotm", [128, 128], BF16)

    d_out = nc.dram_tensor("out", [D, OWN], F32, kind="ExternalOutput").ap()

    skind = "ExternalOutput" if debug else "Internal"
    d_qscr = nc.dram_tensor("qscr", [4, 128, NH * CH], BF16, kind=skind).ap()
    d_oscr = nc.dram_tensor("oscr", [128, NH * OWN], BF16, kind=skind).ap()
    d_x2scr = nc.dram_tensor("x2scr", [128, NKC * OWN], F32, kind=skind).ap()
    if debug:
        d_dxn = nc.dram_tensor("dxn", [2, 128, NKC * 1024], BF16, kind="ExternalOutput").ap()
        d_dk = nc.dram_tensor("dk", [128, KVH * S], BF16, kind="ExternalOutput").ap()
        d_dv = nc.dram_tensor("dv", [128, 16 * 1024], BF16, kind="ExternalOutput").ap()
        d_dw = nc.dram_tensor("dw", [128, 64], F32, kind="ExternalOutput").ap()
        d_dyn = nc.dram_tensor("dyn", [128, NKC * OWN], BF16, kind="ExternalOutput").ap()
        d_dk2 = nc.dram_tensor("dk2", [128, KVH * S], BF16, kind="ExternalOutput").ap()
        d_dv2 = nc.dram_tensor("dv2", [128, 16 * 1024], BF16, kind="ExternalOutput").ap()
        d_dq2 = nc.dram_tensor("dq2", [2, 128, NH * CH], BF16, kind="ExternalOutput").ap()

    sb = nc.alloc_sbuf_tensor
    t_oneb = sb("s_oneb", [128, 128], BF16)
    t_idb = sb("s_idb", [128, 128], BF16)
    t_idf = sb("s_idf", [128, 128], F32)
    t_e16 = sb("s_e16", [8, 128], BF16)
    t_rk = sb("s_rk", [128, 64], F32)
    t_bb = sb("s_bb", [128, 32], F32)
    t_br = sb("s_br", [8, 1], F32)
    t_cosk = sb("s_cosk", [128, S], BF16)
    t_sink = sb("s_sink", [128, S], BF16)
    t_cosq = sb("s_cosq", [128, OWN], BF16)
    t_sinq = sb("s_sinq", [128, OWN], BF16)
    t_mask = sb("s_mask", [128, 4 * CH], BF16)
    t_wrb = sb("s_wrb", [128, NKC * 8], BF16)
    t_rotm = sb("s_rotm", [128, 128], BF16)
    t_t8 = sb("s_t8", [128, 64], F32)
    t_eps = sb("s_eps", [128, 1], F32)
    t_onef = sb("s_onef", [128, 128], F32)
    t_tau = sb("s_tau", [128, 8], F32)
    t_p6t = sb("s_p6t", [128, OWN], F32)

    KB = 1024
    base = (nc.sbuf_base + 31) // 32 * 32
    R1 = base
    R2 = R1 + 64 * KB
    R3 = R2 + 32 * KB
    R4a = R3 + 32 * KB
    R4b = R4a + 16 * KB
    TOP = R4b + 40 * KB
    assert TOP <= nc.sbuf_top - 512, (base, TOP, nc.sbuf_top)

    at = nc.alloc_sbuf_tensor_at
    t_xnh = at("xnh", [128, NKC * 1024], BF16, offset=R1)
    t_obuf = at("obuf", [128, NH * OWN], BF16, offset=R1)
    t_ynb = at("ynb", [128, NKC * OWN], BF16, offset=R1)
    t_kbuf = at("kbuf", [128, KVH * S], BF16, offset=R2)
    t_x2k = [at(f"x2k{i}", [128, OWN], F32, offset=R2 + i * 4 * KB) for i in range(2)]
    t_fo = [at(f"fo{i}", [128, 512], F32, offset=R2 + 8 * KB + i * 2 * KB) for i in range(2)]
    t_outb = [at(f"outb{i}", [128, 512], F32, offset=R2 + 12 * KB + i * 2 * KB) for i in range(2)]
    t_at = [at(f"at{i}", [128, 512], F32, offset=R2 + 16 * KB + i * 2 * KB) for i in range(2)]
    t_x2c = [at(f"x2c{i}", [128, 512], F32, offset=R2 + 20 * KB + i * 2 * KB) for i in range(2)]
    t_xoc = [at(f"xoc{i}", [128, 512], F32, offset=R2 + 24 * KB + i * 2 * KB) for i in range(2)]
    t_vbuf = at("vbuf", [128, 16 * 1024], BF16, offset=R3)
    t_xb2 = [at(f"xb2{i}", [128, 512], BF16, offset=R3 + i * KB) for i in range(2)]
    t_sq2 = [at(f"sq2{i}", [128, 512], BF16, offset=R3 + 2 * KB + i * KB) for i in range(2)]
    t_rstd2 = at("rstd2", [128, OWN], F32, offset=R3 + 4 * KB)
    t_b2 = at("b2", [128, OWN], F32, offset=R3 + 8 * KB)
    t_ha = at("ha", [128, OWN], BF16, offset=R3 + 12 * KB)
    t_g = at("gbuf", [128, OWN], BF16, offset=R3 + 14 * KB)
    t_wB = at("wBb", [128, OWN], BF16, offset=R3 + 16 * KB)
    t_lgT = at("lgT", [8, OWN], F32, offset=R3 + 18 * KB)
    t_lgtok = at("lgtok", [128, 64], F32, offset=R3 + 22 * KB)
    t_wtok = at("wtok", [128, 64], F32, offset=R3 + 22 * KB + 256)
    t_srt = at("srt", [128, 64], F32, offset=R3 + 24 * KB)
    t_srtmp = at("srtmp", [128, 8], F32, offset=R3 + 22 * KB + 512)
    t_wT = at("wTt", [8, OWN], BF16, offset=R3 + 26 * KB)
    t_wbuf = [at(f"wbuf{i}", [128, NKC * 128], BF16, offset=R4a + i * 8 * KB)
              for i in range(2)]
    t_pt = [at(f"pt{i}", [128, 16 * CH], BF16, offset=R4a + i * 8 * KB)
            for i in range(2)]
    t_xc = [at(f"xc{i}", [128, 512], F32, offset=R4b + i * 2 * KB) for i in range(4)]
    t_sqf = [at(f"sqf{i}", [128, 512], F32, offset=R4b + 8 * KB + i * 2 * KB)
             for i in range(2)]
    LNO = R4b + 12 * KB
    t_mS = at("mS", [128, 512], F32, offset=LNO)
    t_qS = at("qS", [128, 512], F32, offset=LNO + 2 * KB)
    t_var = at("var", [128, 512], F32, offset=LNO + 4 * KB)
    t_rstd = at("rstd", [128, 512], F32, offset=LNO + 6 * KB)
    t_bS = at("bS", [128, 512], F32, offset=LNO + 8 * KB)
    QRO = R4b + 22 * KB
    t_qr = [at(f"qr{i}", [128, CH], BF16, offset=QRO + i * 512) for i in range(2)]
    t_q1 = [at(f"q1{i}", [128, CH], BF16, offset=QRO + KB + i * 512) for i in range(2)]
    t_q2 = [at(f"q2{i}", [128, CH], BF16, offset=QRO + 2 * KB + i * 512) for i in range(2)]
    t_qf = [at(f"qf{i}", [128, CH], BF16, offset=QRO + 3 * KB + i * 512) for i in range(2)]
    KRO = R4b + 26 * KB
    t_kr = [at(f"kr{i}", [128, 512], BF16, offset=KRO + i * KB) for i in range(2)]
    t_k1 = [at(f"k1{i}", [128, 512], BF16, offset=KRO + 2 * KB + i * KB) for i in range(2)]
    t_k2 = [at(f"k2{i}", [128, 512], BF16, offset=KRO + 4 * KB + i * KB) for i in range(2)]
    t_vt = [at(f"vt{i}", [128, 512], F32, offset=KRO + 6 * KB + i * 2 * KB) for i in range(2)]
    t_qv = [at(f"qv{i}", [128, NH * CH], BF16, offset=R4b + i * 16 * KB) for i in range(2)]
    ATO = R4b + 32 * KB
    t_ss = [at(f"ss{i}", [128, CH], F32, offset=ATO + i * KB) for i in range(2)]
    t_ot = [at(f"ott{i}", [128, CH], F32, offset=ATO + 2 * KB + i * KB) for i in range(2)]
    t_rs = [at(f"rs{i}", [128, CH], F32, offset=ATO + 4 * KB + i * KB) for i in range(2)]
    t_of = [at(f"of{i}", [128, CH], BF16, offset=ATO + 6 * KB + i * 512) for i in range(2)]
    t_lab = at("lab", [128, NKC * ER], BF16, offset=R4b)
    t_lbb = at("lbb", [ER, D], BF16, offset=R4b + 8 * KB)

    bld = Builder(nc)
    op = bld.op
    sH, sG, sP, sA, sV = (bld.sems[e] for e in ("sync", "gpsimd", "tensor",
                                                "scalar", "vector"))
    sS = sH

    def ld(dst, src, reads=(), inc=True):
        return op("sync", lambda e, o=dst, i=src: e.dma_start(out=o, in_=i),
                  reads=list(reads), writes=[dst.tensor if hasattr(dst, "tensor") else dst],
                  inc=inc)

    def T(x):
        return x.tensor if hasattr(x, "tensor") else x

    xqctr = {"sync": 0, "scalar": 0}

    def dma(eng, dst_ap, src_ap, rd, wr, stream=None):
        return op(eng, lambda e, o=dst_ap, i=src_ap: e.dma_start(out=o, in_=i),
                  reads=rd, writes=wr, is_dma=True, stream=stream)

    def dma_x(eng, dst_ap, src_ap, rd, wr):
        c = xqctr[eng]
        xqctr[eng] = c + 1
        return dma(eng, dst_ap, src_ap, rd, wr, stream=f"x_{eng}_{c % 2}")

    from collections import defaultdict as _dd
    _sctr = _dd(int)

    def dma_p(eng, dst_ap, src_ap, rd, wr, sbase):
        c = _sctr[sbase]
        _sctr[sbase] = c + 1
        return dma(eng, dst_ap, src_ap, rd, wr, stream=f"{sbase}{c % 2}")

    from concourse.hw_specs import get_activation_tables
    _set_id = None
    for idx, (name, funcs) in enumerate(get_activation_tables(nc.m.arch).items()):
        if AF.Exp in funcs and AF.Ln in funcs and AF.Copy in funcs:
            _set_id = idx
            break
    assert _set_id is not None

    def preload(e, _sid=_set_id):
        return e.add_instruction(mybir.InstLoadActFuncSet(
            name=nc.get_next_instruction_name(), act_func_set_id=_sid,
            ins=[], outs=[]))
    op("scalar", preload, inc=False)

    for dst, src in ((t_oneb, d_oneb), (t_idf, d_idf)):
        dma("sync", dst[:], src, rd=[], wr=[dst], stream="const")
    late_consts = ((t_idb, d_idb), (t_e16, d_e16), (t_rk, d_rk), (t_bb, d_bb),
                   (t_br, d_br), (t_cosk, d_cosk), (t_sink, d_sink),
                   (t_cosq, d_cosq), (t_sinq, d_sinq), (t_mask, d_mask),
                   (t_rotm, d_rotm))

    from contextlib import ExitStack

    op("vector", lambda e: e.memset(t_eps[:], EPS), reads=[], writes=[t_eps])
    op("vector", lambda e: e.memset(t_onef[:], 1.0), reads=[], writes=[t_onef])

    with ExitStack() as px:
        ps_sx = px.enter_context(nc.psum_tensor("ps_sx", [128, 512], F32))
        ps_sq = px.enter_context(nc.psum_tensor("ps_sq", [128, 512], F32))
        ps_qa = [px.enter_context(nc.psum_tensor(f"ps_qa{i}", [128, CH], F32))
                 for i in range(2)]
        ps_ka = [px.enter_context(nc.psum_tensor(f"ps_ka{i}", [128, 512], F32))
                 for i in range(2)]
        ps_rot = [px.enter_context(nc.psum_tensor(f"ps_rot{i}", [128, 512], F32))
                  for i in range(2)]

        def q_eng(q, kc):
            if not USE_SCALARQ:
                return "sync"
            if q == "sync":
                return "sync"
            if q == "scalarq":
                return "scalar"
            if q == "split":
                return "sync" if kc % 2 == 0 else "scalar"
            return "scalar" if kc % 2 == 0 else "sync"

        def emit_stats_chunk(half, tg2, kc, q, slots=(0, 1, 2, 3)):
            tg = half * 2 + tg2
            sl, b2 = slots[kc % len(slots)], kc % 2
            dma_x(q_eng(q, kc), t_xc[sl][:], d_xt[tg, :, kc, :],
                  rd=["dram_x"], wr=[t_xc[sl]])
            op("vector",
               lambda e, o=t_sqf[b2], i=t_xc[sl]:
               e.tensor_tensor(o[:], i[:], i[:], ALU.mult),
               reads=[t_xc[sl]], writes=[t_sqf[b2]])
            op("tensor",
               lambda e, r=t_xc[sl], st_=(kc == 0), sp=(kc == NKC - 1):
               e.matmul(ps_sx[:], t_onef[:], r[:], start=st_, stop=sp),
               reads=[t_xc[sl], t_onef], writes=[ps_sx], inc=True)
            op("tensor",
               lambda e, r=t_sqf[b2], st_=(kc == 0), sp=(kc == NKC - 1):
               e.matmul(ps_sq[:], t_onef[:], r[:], start=st_, stop=sp),
               reads=[t_sqf[b2], t_onef], writes=[ps_sq], inc=True)

        def emit_finalize():
            op("scalar", lambda e: e.activation(t_mS[:], ps_sx[:], AF.Copy, scale=INV_D),
               reads=[ps_sx], writes=[t_mS])
            op("scalar", lambda e: e.activation(t_qS[:], ps_sq[:], AF.Copy, scale=INV_D),
               reads=[ps_sq], writes=[t_qS])
            op("vector", lambda e: e.tensor_tensor(t_var[:], t_mS[:], t_mS[:], ALU.mult),
               reads=[t_mS], writes=[t_var])
            op("vector", lambda e: e.tensor_tensor(t_var[:], t_qS[:], t_var[:], ALU.subtract),
               reads=[t_qS, t_var], writes=[t_var])
            op("scalar", lambda e: e.activation(t_rstd[:], t_var[:], AF.Ln, bias=t_eps[:]),
               reads=[t_var, t_eps], writes=[t_rstd])
            op("scalar", lambda e: e.activation(t_rstd[:], t_rstd[:], AF.Exp, scale=-0.5),
               reads=[t_rstd], writes=[t_rstd])
            op("vector", lambda e: e.tensor_tensor(t_bS[:], t_mS[:], t_rstd[:], ALU.mult),
               reads=[t_mS, t_rstd], writes=[t_bS])
            op("vector", lambda e: e.tensor_scalar(t_bS[:], t_bS[:], -1.0, None, ALU.mult),
               reads=[t_bS], writes=[t_bS])

        def emit_normalize_chunk(half, tg2, kc, q="sync", slots=(0, 1, 2, 3)):
            tg = half * 2 + tg2
            sl = slots[kc % len(slots)]
            xnsl = t_xnh[:, kc * 1024 + tg2 * 512:kc * 1024 + tg2 * 512 + 512]
            xkey = f"xn{kc}_{tg2}"
            dma_x(q_eng(q, kc), t_xc[sl][:], d_xt[tg, :, kc, :],
                  rd=["dram_x"], wr=[t_xc[sl]])
            op("vector",
               lambda e, i=t_xc[sl]:
               e.tensor_tensor(t_var[:], i[:], t_rstd[:], ALU.mult),
               reads=[t_xc[sl], t_rstd], writes=[t_var])
            op("vector",
               lambda e, o=xnsl:
               e.tensor_tensor(o, t_var[:], t_bS[:], ALU.add),
               reads=[t_var, t_bS], writes=[xkey])

        def emit_q_group(h, c2, wb):
            pa = ps_qa[c2]
            for kc in range(NKC):
                op("tensor",
                   lambda e, pa=pa, wb=wb, kc=kc, c2=c2,
                   st_=(kc == 0), sp=(kc == NKC - 1):
                   e.matmul(pa[:], wb[:, kc * 128:(kc + 1) * 128],
                            t_xnh[:, kc * 1024 + c2 * 512:
                                  kc * 1024 + c2 * 512 + CH],
                            start=st_, stop=sp),
                   reads=[wb, f"xn{kc}_{c2}"], writes=[pa], inc=(kc == NKC - 1))
            op("scalar",
               lambda e, pa=pa, o=t_qr[c2]: e.activation(o[:], pa[:], AF.Copy),
               reads=[pa], writes=[t_qr[c2]])

        def emit_q_rope(half, h, c2):
            v = half * 2 + c2
            cq = v * CH
            op("tensor",
               lambda e, c2=c2:
               e.matmul(ps_rot[c2][:, 0:CH], t_rotm[:], t_qr[c2][:],
                        start=True, stop=True),
               reads=[t_rotm, t_qr[c2]], writes=[ps_rot[c2]])
            op("scalar",
               lambda e, c2=c2: e.activation(t_q2[c2][:], ps_rot[c2][:, 0:CH],
                                             AF.Copy),
               reads=[ps_rot[c2]], writes=[t_q2[c2]])
            op("vector",
               lambda e, c2=c2, cq=cq:
               e.tensor_tensor(t_q1[c2][:], t_qr[c2][:],
                               t_cosq[:, cq:cq + CH], ALU.mult),
               reads=[t_qr[c2], t_cosq], writes=[t_q1[c2]])
            op("vector",
               lambda e, c2=c2, cq=cq:
               e.tensor_tensor(t_q2[c2][:], t_q2[c2][:],
                               t_sinq[:, cq:cq + CH], ALU.mult),
               reads=[t_q2[c2], t_sinq], writes=[t_q2[c2]])
            op("vector",
               lambda e, c2=c2:
               e.tensor_tensor(t_qf[c2][:], t_q1[c2][:], t_q2[c2][:], ALU.add),
               reads=[t_q1[c2], t_q2[c2]], writes=[t_qf[c2]])
            dma("sync", d_qscr[v, :, h * CH:(h + 1) * CH], t_qf[c2][:],
                rd=[t_qf[c2]], wr=["qscr"], stream="qsp")

        def emit_p2(half, hoist, defer_n=0, pops=2, pop_start=0):
            hoist = list(hoist)
            deferred = []
            for oci in range(48):
                if oci >= pop_start:
                    for _ in range(pops):
                        if hoist:
                            hoist.pop(0)()
                wb = t_wbuf[oci % 2]
                wsrc = (d_wq[oci] if oci < 32 else
                        d_wk[oci - 32] if oci < 40 else d_wv[oci - 40])
                dma_p("gpsimd", wb[:].rearrange("p (k o) -> p k o", k=NKC), wsrc,
                      rd=["dram_w"], wr=[wb], sbase="w")

                if oci < 32:
                    h = oci
                    if oci < defer_n:
                        emit_q_group(h, 0, wb)
                        emit_q_rope(half, h, 0)
                        deferred.append((h, wsrc))
                        continue
                    for c2 in range(2):
                        emit_q_group(h, c2, wb)
                    for c2 in range(2):
                        emit_q_rope(half, h, c2)
                else:
                    kvi = oci - 32 if oci < 40 else oci - 40
                    isK = oci < 40
                    for tgl in range(2):
                        pa = ps_ka[tgl]
                        col0 = tgl * 512
                        for kc in range(NKC):
                            op("tensor",
                               lambda e, pa=pa, wb=wb, kc=kc, col0=col0,
                               st_=(kc == 0), sp=(kc == NKC - 1):
                               e.matmul(pa[:], wb[:, kc * 128:(kc + 1) * 128],
                                        t_xnh[:, kc * 1024 + col0:kc * 1024 + col0 + 512],
                                        start=st_, stop=sp),
                               reads=[wb, f"xn{kc}_{tgl}"], writes=[pa], inc=(kc == NKC - 1))
                        op("scalar",
                           lambda e, pa=pa, o=(t_kr[tgl] if isK else t_vt[tgl]):
                           e.activation(o[:], pa[:], AF.Copy),
                           reads=[pa], writes=[t_kr[tgl] if isK else t_vt[tgl]])
                    for tgl in range(2):
                        gcol = half * 1024 + tgl * 512
                        if isK:
                            op("tensor",
                               lambda e, tgl=tgl:
                               e.matmul(ps_rot[tgl][:], t_rotm[:], t_kr[tgl][:],
                                        start=True, stop=True),
                               reads=[t_rotm, t_kr[tgl]], writes=[ps_rot[tgl]])
                            op("scalar",
                               lambda e, tgl=tgl: e.activation(t_k2[tgl][:], ps_rot[tgl][:],
                                                               AF.Copy),
                               reads=[ps_rot[tgl]], writes=[t_k2[tgl]])
                            op("vector",
                               lambda e, tgl=tgl, gcol=gcol:
                               e.tensor_tensor(t_k1[tgl][:], t_kr[tgl][:],
                                               t_cosk[:, gcol:gcol + 512], ALU.mult),
                               reads=[t_kr[tgl], t_cosk], writes=[t_k1[tgl]])
                            op("vector",
                               lambda e, tgl=tgl, gcol=gcol:
                               e.tensor_tensor(t_k2[tgl][:], t_k2[tgl][:],
                                               t_sink[:, gcol:gcol + 512], ALU.mult),
                               reads=[t_k2[tgl], t_sink], writes=[t_k2[tgl]])
                            op("vector",
                               lambda e, tgl=tgl, kvi=kvi, gcol=gcol:
                               e.tensor_tensor(
                                   t_kbuf[:, kvi * S + gcol:kvi * S + gcol + 512],
                                   t_k1[tgl][:], t_k2[tgl][:], ALU.add),
                               reads=[t_k1[tgl], t_k2[tgl]], writes=[t_kbuf])
                        else:
                            for i4 in range(4):
                                tt = half * 8 + tgl * 4 + i4
                                pv = ps_rot[i4 % 2]
                                op("tensor",
                                   lambda e, tgl=tgl, i4=i4, pv=pv:
                                   e.transpose(pv[:, 0:128], t_vt[tgl][:, i4 * 128:(i4 + 1) * 128],
                                               t_idf[:]),
                                   reads=[t_vt[tgl], t_idf], writes=[pv])
                                op("scalar",
                                   lambda e, pv=pv, tt=tt, kvi=kvi:
                                   e.activation(t_vbuf[:, tt * 1024 + kvi * 128:
                                                       tt * 1024 + kvi * 128 + 128],
                                                pv[:, 0:128], AF.Copy),
                                   reads=[pv], writes=[t_vbuf])
            for j, (h, wsrc) in enumerate(deferred):
                wb = t_wbuf[j % 2]
                dma_p("gpsimd", wb[:].rearrange("p (k o) -> p k o", k=NKC), wsrc,
                      rd=["dram_w"], wr=[wb], sbase="w")
                emit_q_group(h, 1, wb)
                emit_q_rope(half, h, 1)

        if SEQ_LN:
            for half in range(2):
                for tg2 in range(2):
                    for kc in range(NKC):
                        emit_stats_chunk(half, tg2, kc, "split")
                    emit_finalize()
                    if half == 0 and tg2 == 0:
                        for dst, csrc in late_consts:
                            dma("sync", dst[:], csrc, rd=[], wr=[dst], stream="const")
                    for kc in range(NKC):
                        emit_normalize_chunk(half, tg2, kc, "split")
                emit_p2(half, [], defer_n=(DEFER0 if half == 0 else DEFER1))
        else:
            for kc in range(NKC):
                emit_stats_chunk(0, 0, kc, "split")
            emit_finalize()
            for dst, csrc in late_consts:
                dma("sync", dst[:], csrc, rd=[], wr=[dst], stream="const")
            for kc in range(NKC):
                emit_stats_chunk(0, 1, kc, "split", slots=(0, 1))
                emit_normalize_chunk(0, 0, kc, "tilps", slots=(2, 3))
            emit_finalize()
            for kc in range(NKC):
                emit_normalize_chunk(0, 1, kc, "split")
            hoist = [(lambda kc=kc: emit_stats_chunk(1, 0, kc, "split")) for kc in range(NKC)]
            emit_p2(0, hoist, defer_n=DEFER0, pops=4, pop_start=40)
            emit_finalize()
            for kc in range(NKC):
                emit_stats_chunk(1, 1, kc, "split", slots=(0, 1))
                emit_normalize_chunk(1, 0, kc, "tilps", slots=(2, 3))
            emit_finalize()
            for kc in range(NKC):
                emit_normalize_chunk(1, 1, kc, "split")
            emit_p2(1, [], defer_n=DEFER1)

    if debug:
        dma("sync", d_dxn[1], t_xnh[:], rd=[t_xnh], wr=["ddxn"], stream="dbg")
        dma("sync", d_dk, t_kbuf[:], rd=[t_kbuf], wr=["ddk"], stream="dbg")
        dma("sync", d_dv, t_vbuf[:], rd=[t_vbuf], wr=["ddv"], stream="dbg")

    bld.fence()
    with ExitStack() as px:
        ps_st = [px.enter_context(nc.psum_tensor(f"ps_st{i}", [128, 512], F32))
                 for i in range(4)]
        ps_sm = [px.enter_context(nc.psum_tensor(f"ps_sm{i}", [128, CH], F32))
                 for i in range(2)]
        ps_ot = [px.enter_context(nc.psum_tensor(f"ps_ot{i}", [128, CH], F32))
                 for i in range(2)]

        LEAD = 3
        pairs = []
        for v in range(4):
            for h in range(NH):
                G = 2 * v + 2
                for g in range(G):
                    pairs.append((v, h, g, G))
        qv_of = {}
        for v in range(4):
            qv_of[v] = t_qv[v % 2]

        hv_of = {}
        hv = 0
        for v in range(4):
            for h in range(NH):
                hv_of[(v, h)] = hv
                hv += 1

        def emit_st(idx):
            v, h, g, G = pairs[idx]
            qv = qv_of[v]
            kvh = h // GROUPS
            pst = ps_st[idx % 4]
            for i2 in range(2):
                kb = 2 * g + i2
                op("tensor",
                   lambda e, pst=pst, i2=i2, kb=kb, kvh=kvh, h=h, qv=qv:
                   e.matmul(pst[:, i2 * CH:(i2 + 1) * CH],
                            t_kbuf[:, kvh * S + kb * 128:kvh * S + kb * 128 + 128],
                            qv[:, h * CH:(h + 1) * CH],
                            start=True, stop=True),
                   reads=[t_kbuf, qv], writes=[pst], inc=(i2 == 1))

        def emit_consume(idx):
            v, h, g, G = pairs[idx]
            kvh = h // GROUPS
            hv = hv_of[(v, h)]
            ptb = t_pt[hv % 2]
            pss, pso = ps_sm[hv % 2], ps_ot[hv % 2]
            nkb = 2 * G
            pst = ps_st[idx % 4]
            pk = lambda kb, par=hv % 2: f"ptsl{par}_{kb}"
            op("scalar",
               lambda e, pst=pst, ptb=ptb, g=g:
               e.activation(ptb[:, g * 512:(g + 1) * 512], pst[:], AF.Exp,
                            scale=SCALE),
               reads=[pst], writes=[pk(2 * g), pk(2 * g + 1)])
            for i2 in range(2):
                kb = 2 * g + i2
                mi = kb - 4 * v
                if mi >= 0:
                    op("vector",
                       lambda e, ptb=ptb, kb=kb, mi=mi:
                       e.tensor_tensor(ptb[:, kb * CH:(kb + 1) * CH],
                                       ptb[:, kb * CH:(kb + 1) * CH],
                                       t_mask[:, mi * CH:(mi + 1) * CH], ALU.mult),
                       reads=[pk(kb), t_mask], writes=[pk(kb)])
                op("tensor",
                   lambda e, pss=pss, ptb=ptb, kb=kb,
                   st_=(kb == 0), sp=(kb == nkb - 1):
                   e.matmul(pss[:], t_oneb[:], ptb[:, kb * CH:(kb + 1) * CH],
                            start=st_, stop=sp),
                   reads=[t_oneb, pk(kb)], writes=[pss], inc=(kb == nkb - 1))
                op("tensor",
                   lambda e, pso=pso, ptb=ptb, kb=kb, kvh=kvh,
                   st_=(kb == 0), sp=(kb == nkb - 1):
                   e.matmul(pso[:],
                            t_vbuf[:, kb * 1024 + kvh * 128:kb * 1024 + kvh * 128 + 128],
                            ptb[:, kb * CH:(kb + 1) * CH],
                            start=st_, stop=sp),
                   reads=[t_vbuf, pk(kb)], writes=[pso], inc=(kb == nkb - 1))
            if g == G - 1:
                def tail(pss=pss, pso=pso, hv=hv, h=h, v=v):
                    ssb, otb, rsb = t_ss[hv % 2], t_ot[hv % 2], t_rs[hv % 2]
                    op("scalar", lambda e: e.activation(ssb[:], pss[:], AF.Ln),
                       reads=[pss], writes=[ssb])
                    op("scalar", lambda e: e.activation(rsb[:], ssb[:], AF.Exp,
                                                        scale=-1.0),
                       reads=[ssb], writes=[rsb])
                    op("scalar", lambda e: e.activation(otb[:], pso[:], AF.Copy),
                       reads=[pso], writes=[otb])
                    op("vector", lambda e:
                       e.tensor_tensor(t_obuf[:, h * OWN + v * CH:h * OWN + v * CH + CH],
                                       otb[:], rsb[:], ALU.mult),
                       reads=[otb, rsb], writes=[t_obuf])
                tail_q.append((idx, tail))

        NP = len(pairs)
        start_idx = {}
        acc = 0
        for v in range(4):
            start_idx[v] = acc
            acc += NH * (2 * v + 2)
        triggers = {start_idx[v - 1] + NH * (2 * (v - 1) + 2) // 2: v
                    for v in range(1, 4)}
        dma_p("sync", qv_of[0][:], d_qscr[0], rd=["qscr"], wr=[qv_of[0]], sbase="qv")
        tail_q = []
        for idx in range(min(LEAD, NP)):
            emit_st(idx)
        for idx in range(NP):
            v_pre = triggers.get(idx)
            if v_pre is not None:
                dma_p("sync", qv_of[v_pre][:], d_qscr[v_pre],
                      rd=["qscr"], wr=[qv_of[v_pre]], sbase="qv")
            if idx + LEAD < NP:
                emit_st(idx + LEAD)
            emit_consume(idx)
            while tail_q and tail_q[0][0] <= idx - 2:
                tail_q.pop(0)[1]()
        while tail_q:
            tail_q.pop(0)[1]()
    bld.fence()
    with ExitStack() as px:
        ps_wo = [px.enter_context(nc.psum_tensor(f"ps_wo{i}", [128, 512], F32))
                 for i in range(2)]
        ps_s2x = [px.enter_context(nc.psum_tensor(f"ps_s2x{i}", [128, 512], F32))
                  for i in range(2)]
        ps_s2q = [px.enter_context(nc.psum_tensor(f"ps_s2q{i}", [128, 512], F32))
                  for i in range(2)]

        stat_mm_pending = []
        for oc in range(32):
            wb = t_wbuf[oc % 2]
            dma_p("gpsimd", wb[:].rearrange("p (k o) -> p k o", k=NKC), d_wo[oc], rd=["dram_w"], wr=[wb], sbase="w")
            for tg in range(2):
                pw = ps_wo[tg]
                for kc in range(NKC):
                    op("tensor",
                       lambda e, pw=pw, wb=wb, kc=kc, tg=tg,
                       st_=(kc == 0), sp=(kc == NKC - 1):
                       e.matmul(pw[:], wb[:, kc * 128:(kc + 1) * 128],
                                t_obuf[:, kc * 1024 + tg * 512:kc * 1024 + tg * 512 + 512],
                                start=st_, stop=sp),
                       reads=[wb, t_obuf], writes=[pw], inc=(kc == NKC - 1))
                for th in stat_mm_pending:
                    th()
                stat_mm_pending = []
                atb, x2b, xob = t_at[tg], t_x2c[tg], t_xoc[tg]
                op("scalar", lambda e, pw=pw, atb=atb: e.activation(atb[:], pw[:], AF.Copy),
                   reads=[pw], writes=[atb])
                dma_p("sync", xob[:], d_xo[:, oc, tg * 512:(tg + 1) * 512],
                      rd=["dram_xo"], wr=[xob], sbase="xo")
                op("vector", lambda e, atb=atb, xob=xob, x2b=x2b:
                   e.tensor_tensor(x2b[:], atb[:], xob[:], ALU.add),
                   reads=[atb, xob], writes=[x2b])
                dma_p("sync", d_x2scr[:, oc * OWN + tg * 512:oc * OWN + tg * 512 + 512],
                      x2b[:], rd=[x2b], wr=["x2scr"], sbase="x2s")
                xb2, sq2 = t_xb2[tg], t_sq2[tg]
                op("scalar", lambda e, x2b=x2b, xb2=xb2: e.activation(xb2[:], x2b[:], AF.Copy),
                   reads=[x2b], writes=[xb2])
                op("vector", lambda e, xb2=xb2, sq2=sq2:
                   e.tensor_tensor(sq2[:], xb2[:], xb2[:], ALU.mult),
                   reads=[xb2], writes=[sq2])

                def mk_stats(xb2=xb2, sq2=sq2, tg=tg, oc=oc):
                    def th():
                        op("tensor",
                           lambda e, st_=(oc == 0), sp=(oc == 31):
                           e.matmul(ps_s2x[tg][:], t_oneb[:], xb2[:], start=st_, stop=sp),
                           reads=[t_oneb, xb2], writes=[ps_s2x[tg]], inc=(oc == 31))
                        op("tensor",
                           lambda e, st_=(oc == 0), sp=(oc == 31):
                           e.matmul(ps_s2q[tg][:], t_oneb[:], sq2[:], start=st_, stop=sp),
                           reads=[t_oneb, sq2], writes=[ps_s2q[tg]], inc=(oc == 31))
                    return th
                stat_mm_pending.append(mk_stats())
        for th in stat_mm_pending:
            th()

        for tg in range(2):
            rsl = t_rstd2[:, tg * 512:(tg + 1) * 512]
            bsl = t_b2[:, tg * 512:(tg + 1) * 512]
            op("scalar", lambda e, tg=tg: e.activation(t_mS[:], ps_s2x[tg][:], AF.Copy,
                                                       scale=INV_D),
               reads=[ps_s2x[tg]], writes=[t_mS])
            op("scalar", lambda e, tg=tg: e.activation(t_qS[:], ps_s2q[tg][:], AF.Copy,
                                                       scale=INV_D),
               reads=[ps_s2q[tg]], writes=[t_qS])
            op("vector", lambda e: e.tensor_tensor(t_var[:], t_mS[:], t_mS[:], ALU.mult),
               reads=[t_mS], writes=[t_var])
            op("vector", lambda e: e.tensor_tensor(t_var[:], t_qS[:], t_var[:], ALU.subtract),
               reads=[t_qS, t_var], writes=[t_var])
            op("scalar", lambda e: e.activation(t_rstd[:], t_var[:], AF.Ln, bias=t_eps[:]),
               reads=[t_var, t_eps], writes=[t_rstd])
            op("scalar", lambda e, rsl=rsl: e.activation(rsl, t_rstd[:], AF.Exp, scale=-0.5),
               reads=[t_rstd], writes=[t_rstd2])
            op("vector", lambda e, rsl=rsl: e.tensor_tensor(t_bS[:], t_mS[:], rsl, ALU.mult),
               reads=[t_mS, t_rstd2], writes=[t_bS])
            op("vector", lambda e, bsl=bsl: e.tensor_scalar(bsl, t_bS[:], -1.0, None, ALU.mult),
               reads=[t_bS], writes=[t_b2])


    bld.fence()
    with ExitStack() as px:
        ps_lg = [px.enter_context(nc.psum_tensor(f"ps_lg{i}", [8, 512], F32))
                 for i in range(2)]
        ps_ha = [px.enter_context(nc.psum_tensor(f"ps_ha{i}", [128, 512], F32))
                 for i in range(2)]
        ps_tp8 = [px.enter_context(nc.psum_tensor(f"ps_tp8{i}", [128, 8], F32))
                  for i in range(2)]
        ps_tpw = px.enter_context(nc.psum_tensor("ps_tpw", [8, 128], F32))
        ps_wb2 = [px.enter_context(nc.psum_tensor("ps_wb2x", [128, 512], F32))] * 2

        dma("gpsimd", t_wrb[:].rearrange("p (k e) -> p k e", k=NKC), d_wr, stream="w6r",
            rd=["dram_w"], wr=[t_wrb])
        dma("gpsimd", t_lab[:].rearrange("p (k r) -> p k r", k=NKC), d_la, stream="w6a",
            rd=["dram_w"], wr=[t_lab])
        dma("gpsimd", t_lbb[:], d_lb, rd=["dram_w"], wr=[t_lbb], stream="w6b")

        for kc in range(NKC):
            xk = t_x2k[kc % 2]
            dma_p("sync", xk[:], d_x2scr[:, kc * OWN:(kc + 1) * OWN], rd=["x2scr"], wr=[xk], sbase="x2y")
            op("vector", lambda e, xk=xk: e.tensor_tensor(t_p6t[:], xk[:], t_rstd2[:], ALU.mult),
               reads=[xk, t_rstd2], writes=[t_p6t])
            op("vector", lambda e, kc=kc:
               e.tensor_tensor(t_ynb[:, kc * OWN:(kc + 1) * OWN], t_p6t[:], t_b2[:], ALU.add),
               reads=[t_p6t, t_b2], writes=[t_ynb])
            for tg in range(2):
                ysl = t_ynb[:, kc * OWN + tg * 512:kc * OWN + tg * 512 + 512]
                op("tensor",
                   lambda e, kc=kc, tg=tg, ysl=ysl, st_=(kc == 0), sp=(kc == NKC - 1):
                   e.matmul(ps_lg[tg][:], t_wrb[:, kc * 8:(kc + 1) * 8], ysl,
                            start=st_, stop=sp),
                   reads=[t_wrb, t_ynb], writes=[ps_lg[tg]], inc=(kc == NKC - 1))
        for tg in range(2):
            op("scalar", lambda e, tg=tg:
               e.activation(t_lgT[:, tg * 512:(tg + 1) * 512], ps_lg[tg][:], AF.Copy),
               reads=[ps_lg[tg]], writes=[t_lgT])
            op("vector", lambda e, tg=tg:
               e.tensor_scalar(t_lgT[:, tg * 512:(tg + 1) * 512],
                               t_lgT[:, tg * 512:(tg + 1) * 512], t_br[:], None, ALU.add),
               reads=[t_lgT, t_br], writes=[t_lgT])
        for ti in range(8):
            pvx = ps_tp8[ti % 2]
            op("tensor", lambda e, ti=ti, pvx=pvx:
               e.transpose(pvx[:], t_lgT[0:8, ti * 128:(ti + 1) * 128], t_idf[0:8, 0:8]),
               reads=[t_lgT, t_idf], writes=[pvx])
            op("scalar", lambda e, ti=ti, pvx=pvx:
               e.activation(t_lgtok[:, ti * 8:(ti + 1) * 8], pvx[:], AF.Copy),
               reads=[pvx], writes=[t_lgtok])
        op("vector", lambda e: e.tensor_copy(t_srt[:], t_lgtok[:]),
           reads=[t_lgtok], writes=[t_srt])

        def s_sl(i):
            return t_srt[:].rearrange("p (t e) -> p t e", e=8)[:, :, i]
        for (i, j) in SORT8:
            op("vector", lambda e, i=i, j=j:
               e.tensor_tensor(t_srtmp[:], s_sl(i), s_sl(j), ALU.min),
               reads=[t_srt], writes=[t_srtmp])
            op("vector", lambda e, i=i, j=j:
               e.tensor_tensor(s_sl(i), s_sl(i), s_sl(j), ALU.max),
               reads=[t_srt], writes=[t_srt])
            op("vector", lambda e, j=j:
               e.tensor_copy(s_sl(j), t_srtmp[:]),
               reads=[t_srtmp], writes=[t_srt])
        for k in range(1, 8):
            op("vector", lambda e, k=k:
               e.tensor_tensor(s_sl(k), s_sl(k), s_sl(k - 1), ALU.add),
               reads=[t_srt], writes=[t_srt])
        op("vector", lambda e:
           e.tensor_scalar(t_t8[:], t_srt[:], -1.0, None, ALU.add),
           reads=[t_srt], writes=[t_t8])
        op("vector", lambda e:
           e.tensor_tensor(t_t8[:], t_t8[:], t_rk[:], ALU.mult),
           reads=[t_t8, t_rk], writes=[t_t8])
        op("vector", lambda e:
           e.tensor_reduce(t_tau[:], t_t8[:].rearrange("p (t e) -> p t e", e=8),
                           AX.X, ALU.max),
           reads=[t_t8], writes=[t_tau])
        for ti in range(8):
            op("vector", lambda e, ti=ti:
               e.tensor_scalar(t_wtok[:, ti * 8:(ti + 1) * 8], t_lgtok[:, ti * 8:(ti + 1) * 8],
                               t_tau[:, ti:ti + 1], 0.0, ALU.subtract, ALU.max),
               reads=[t_lgtok, t_tau], writes=[t_wtok])
        for tg in range(2):
            for kc in range(NKC):
                ysl = t_ynb[:, kc * OWN + tg * 512:kc * OWN + tg * 512 + 512]
                op("tensor",
                   lambda e, kc=kc, tg=tg, ysl=ysl, st_=(kc == 0), sp=(kc == NKC - 1):
                   e.matmul(ps_ha[tg][:], t_lab[:, kc * ER:(kc + 1) * ER], ysl,
                            start=st_, stop=sp),
                   reads=[t_lab, t_ynb], writes=[ps_ha[tg]], inc=(kc == NKC - 1))
            op("scalar", lambda e, tg=tg:
               e.activation(t_ha[:, tg * 512:(tg + 1) * 512], ps_ha[tg][:], AF.Copy),
               reads=[ps_ha[tg]], writes=[t_ha])
        for ti in range(8):
            op("tensor", lambda e, ti=ti:
               e.transpose(ps_tpw[:], t_wtok[:, ti * 8:(ti + 1) * 8], t_idf[:]),
               reads=[t_wtok, t_idf], writes=[ps_tpw])
            op("scalar", lambda e, ti=ti:
               e.activation(t_wT[:, ti * 128:(ti + 1) * 128], ps_tpw[:], AF.Copy),
               reads=[ps_tpw], writes=[t_wT])
        for tg in range(2):
            op("tensor", lambda e, tg=tg:
               e.matmul(ps_wb2[tg][:], t_e16[:], t_wT[:, tg * 512:(tg + 1) * 512],
                        start=True, stop=True),
               reads=[t_e16, t_wT], writes=[ps_wb2[tg]])
            op("scalar", lambda e, tg=tg:
               e.activation(t_wB[:, tg * 512:(tg + 1) * 512], ps_wb2[tg][:], AF.Copy),
               reads=[ps_wb2[tg]], writes=[t_wB])
        op("vector", lambda e: e.tensor_tensor(t_g[:], t_ha[:], t_wB[:], ALU.mult),
           reads=[t_ha, t_wB], writes=[t_g])
        if debug:
            dma("sync", d_dyn, t_ynb[:], rd=[t_ynb], wr=["ddyn"], stream="dbg")
            dma("sync", d_dw, t_wtok[:], rd=[t_wtok], wr=["ddw"], stream="dbg")

    bld.fence(("tensor", "scalar"))
    with ExitStack() as px:
        ps_fb = [px.enter_context(nc.psum_tensor(f"ps_fb{i}", [128, 512], F32))
                 for i in range(2)]
        for oc in range(32):
            wb = t_wbuf[oc % 2]
            dma_p("gpsimd", wb[:].rearrange("p (k o) -> p k o", k=NKC), d_wb[oc], rd=["dram_w"], wr=[wb], sbase="w")
            for tg in range(2):
                pf = ps_fb[tg]
                for kc in range(NKC):
                    op("tensor",
                       lambda e, pf=pf, wb=wb, kc=kc, tg=tg, st_=(kc == 0):
                       e.matmul(pf[:], wb[:, kc * 128:(kc + 1) * 128],
                                t_ynb[:, kc * OWN + tg * 512:kc * OWN + tg * 512 + 512],
                                start=st_, stop=False),
                       reads=[wb, t_ynb], writes=[pf], inc=False)
                op("tensor",
                   lambda e, pf=pf, oc=oc, tg=tg:
                   e.matmul(pf[:], t_lbb[:, oc * 128:(oc + 1) * 128],
                            t_g[:, tg * 512:(tg + 1) * 512], start=False, stop=True),
                   reads=[t_lbb, t_g], writes=[pf], inc=True)
                fob, x2b, ob = t_fo[tg], t_x2c[tg], t_outb[tg]
                op("scalar", lambda e, pf=pf, fob=fob:
                   e.activation(fob[:], pf[:], AF.Copy),
                   reads=[pf], writes=[fob])
                dma_p("sync", x2b[:],
                      d_x2scr[:, oc * OWN + tg * 512:oc * OWN + tg * 512 + 512],
                      rd=["x2scr"], wr=[x2b], sbase="x2l")
                op("vector", lambda e, fob=fob, x2b=x2b, ob=ob, oc=oc:
                   e.scalar_tensor_tensor(ob[:], fob[:], t_bb[:, oc:oc + 1], x2b[:],
                                          ALU.add, ALU.add),
                   reads=[fob, x2b, t_bb], writes=[ob])
                dma("sync", d_out[oc * 128:(oc + 1) * 128, tg * 512:(tg + 1) * 512],
                    ob[:], rd=[ob], wr=["dout"], stream="out")

    finals = [(s.h, s.n) for s in bld.sems.values() if s.n > 0]

    def _final_waits(e, finals=tuple(finals)):
        for h, v in finals:
            e.wait_ge(h, v)
    bld.q["sync"].append(_final_waits)

    with nc.Block() as block:
        @block.sync
        def _(eng):
            for f in bld.q["sync"]:
                f(eng)

        @block.gpsimd
        def _(eng):
            for f in bld.q["gpsimd"]:
                f(eng)

        @block.tensor
        def _(eng):
            for f in bld.q["tensor"]:
                f(eng)

        @block.scalar
        def _(eng):
            for f in bld.q["scalar"]:
                f(eng)

        @block.vector
        def _(eng):
            for f in bld.q["vector"]:
                f(eng)

    return nc


_NC_CACHE = {}


def _get_nc(debug=False):
    if debug not in _NC_CACHE:
        _NC_CACHE[debug] = build_nc(debug)
    return _NC_CACHE[debug]


def make_in_maps(inputs):
    x = np.asarray(inputs["L0_attn_hidden_states"])
    wq = np.asarray(inputs["L0_attn_Wq"])
    wk = np.asarray(inputs["L0_attn_Wk"])
    wv = np.asarray(inputs["L0_attn_Wv"])
    wo = np.asarray(inputs["L0_attn_Wo"])
    wbase = np.asarray(inputs["L0_ffn_W_base"])
    bbase = np.asarray(inputs["L0_ffn_b_base"])
    wrout = np.asarray(inputs["L0_ffn_W_router"])
    brout = np.asarray(inputs["L0_ffn_b_router"])
    la = np.asarray(inputs["L0_ffn_lora_A"])
    lb = np.asarray(inputs["L0_ffn_lora_B"])

    wq_t = _tile_w(np.ascontiguousarray(wq.T))
    wk_t = _tile_w(np.ascontiguousarray(wk.T))
    wv_t = _tile_w(np.ascontiguousarray(wv.T))
    wo_t = _tile_w(np.ascontiguousarray(wo.T))
    wb_t = _tile_w(np.ascontiguousarray(wbase.T))
    wr_t = np.ascontiguousarray(wrout.T.reshape(NKC, 128, 8).transpose(1, 0, 2))
    la_t = np.ascontiguousarray(
        la.transpose(1, 0, 2).reshape(D, ER).reshape(NKC, 128, ER).transpose(1, 0, 2))
    lb_t = np.ascontiguousarray(lb.reshape(ER, D))
    bb_t = np.ascontiguousarray(bbase.reshape(32, 128).T)
    br_t = np.ascontiguousarray(brout.reshape(8, 1))
    cosT, sinT = _rope_tables()
    oneb = np.ones((128, 128), np.float32).astype(BF16NP)
    idb = np.eye(128, dtype=np.float32).astype(BF16NP)
    idf = np.eye(128, dtype=np.float32)
    e16 = np.zeros((8, 128), np.float32)
    for e in range(E_):
        e16[e, e * R_:(e + 1) * R_] = 1.0
    e16 = e16.astype(BF16NP)
    rk = np.broadcast_to(np.tile(1.0 / np.arange(1, 9, dtype=np.float32), 8), (128, 64)).copy()
    rotm = np.zeros((128, 128), np.float32)
    for dd in range(64):
        rotm[dd + 64, dd] = -1.0
        rotm[dd, dd + 64] = 1.0
    rotm = rotm.astype(BF16NP)

    in_maps = []
    for c in range(8):
        b, j = c // 2, c % 2
        pc = _perm_cols(j)
        oc_ = _own_cols(j)
        xp = x[b][pc].T
        xt = np.ascontiguousarray(
            xp.reshape(NKC, 128, 4, 512).transpose(2, 1, 0, 3))
        xo = np.ascontiguousarray(
            x[b][oc_].T.reshape(NKC, 128, OWN).transpose(1, 0, 2))
        cosk = np.ascontiguousarray(cosT[:, pc]).astype(BF16NP)
        sink = np.ascontiguousarray(sinT[:, pc]).astype(BF16NP)
        cosq = np.ascontiguousarray(cosT[:, oc_]).astype(BF16NP)
        sinq = np.ascontiguousarray(sinT[:, oc_]).astype(BF16NP)
        in_maps.append(dict(
            xt=xt, xo=xo, wq=wq_t, wk=wk_t, wv=wv_t, wo=wo_t, wb=wb_t,
            wr=wr_t, la=la_t, lb=lb_t, bb=bb_t, br=br_t,
            cosk=cosk, sink=sink, cosq=cosq, sinq=sinq,
            maskt=_mask_tiles(j), oneb=oneb, idb=idb, idf=idf, e16=e16, rk=rk,
            rotm=rotm))
    return in_maps


def kernel(**inputs):
    in_maps = make_in_maps(inputs)
    nc = _get_nc(False)
    res = run_bass_kernel_spmd(nc, in_maps, core_ids=list(range(8))).results
    out = np.empty((B, S, D), np.float32)
    for c in range(8):
        b, j = c // 2, c % 2
        out[b, _own_cols(j), :] = res[c]["out"].T
    return out
